# version 06
# baseline (speedup 1.0000x reference)
import contextlib
import numpy as np
import ml_dtypes
import concourse.bass as bass
import concourse.mybir as mybir
from concourse.bass_utils import run_bass_kernel_spmd

BF16 = ml_dtypes.bfloat16
F32 = mybir.dt.float32
BF = mybir.dt.bfloat16
AF = mybir.ActivationFunctionType
ALU = mybir.AluOpType
AX = mybir.AxisListType

D, H, KD, C = 1024, 16, 64, 64
TQ = 2048
NCH = TQ // C
FT = 512
NFT = TQ // FT
NDB = 8
UW = TQ + 8
LN_EPS, GN_EPS, NORM_EPS = 1e-5, 0.00064, 1e-6


class Sem:
    def __init__(self, h):
        self.h = h
        self.n = 0

    def inc(self, k=1):
        self.n += k
        return self.n


def build_program(dev_dump=False):
    nc = bass.Bass("TRN2", target_bir_lowering=False)
    dt = nc.dram_tensor

    x_in = dt("x_in", [D, TQ], F32, kind="ExternalInput")
    xhalo = dt("xhalo", [128, NDB], F32, kind="ExternalInput")
    WN = ["W_w", "W_v", "W_r", "W_k", "W_a", "W_b", "W_g", "W_o",
          "fW_r", "fW_k", "fW_v"]
    wt = {n: dt("w_" + n, [D, D], F32, kind="ExternalInput") for n in WN}
    VN = ["mx_r", "mx_w", "mx_k", "mx_v", "mx_a", "mx_b", "mx_g",
          "mu_r", "mu_k", "gn_w", "gn_b", "b_o"]
    vecs = {n: dt("v_" + n, [128, NDB], F32, kind="ExternalInput") for n in VN}
    masks_in = dt("masks_in", [128, 4 * C], F32, kind="ExternalInput")
    irep_in = dt("irep_in", [128, C], F32, kind="ExternalInput")
    gnones_in = dt("gnones_in", [128, NDB * 16], F32, kind="ExternalInput")
    gnbc_in = dt("gnbc_in", [16, NDB * 128], F32, kind="ExternalInput")
    csel_in = dt("csel_in", [128, 8], F32, kind="ExternalInput")
    hsel_in = dt("hsel_in", [128, 8], F32, kind="ExternalInput")
    out_t = dt("out_t", [D, TQ], F32, kind="ExternalOutput")

    ks = "ExternalOutput" if dev_dump else "Internal"

    def scr(name, shape, dtyp):
        return dt(name, shape, dtyp, kind=ks)

    s_strip = {n: scr("s_" + n, [128, NDB * TQ], BF) for n in
               ("ah", "bh", "kh", "rh", "araw", "braw", "g", "eA", "eBK",
                "eR")}
    s_v = scr("s_v", [128, (TQ // 128) * D], BF)
    s_sp = scr("s_sp", [128, NDB * TQ], F32)
    s_zt = scr("s_zt", [128, NCH * 8 * C], BF)
    s_mbBR = scr("s_mbBR", [128, NCH * 8 * C], BF)
    s_btok = scr("s_btok", [128, NCH * 8 * C], BF)
    s_o = scr("s_o", [128, NDB * TQ], BF)
    s_x1 = scr("s_x1", [D, TQ], F32)
    s_rsig = scr("s_rsig", [128, NDB * TQ], BF)
    s_kact = scr("s_kact", [128, NDB * TQ], BF)
    sum_loc = dt("sum_loc", [128, 2 * 8 * C], BF)
    sum_all = dt("sum_all", [8 * 128, 2 * 8 * C], BF)
    s_o0 = scr("s_o0", [128, NDB * TQ], BF)
    s_dbg = scr("s_dbg", [128, 8 * 8 * C], F32)
    ln2h_loc = dt("ln2h_loc", [128, 128], BF)
    ln2h_all = dt("ln2h_all", [8 * 128, 128], BF)

    P = {e: [] for e in ("tensor", "vector", "scalar", "gpsimd", "sync")}
    st = contextlib.ExitStack()

    def sb(name, shape, dtyp):
        return st.enter_context(nc.sbuf_tensor(name, shape, dtyp))

    def psum(name, shape, dtyp=F32):
        return st.enter_context(nc.psum_tensor(name, shape, dtyp))

    def sem(name):
        return Sem(st.enter_context(nc.semaphore(name)))

    cvec = {n: sb("c_" + n, [128, NDB], F32) for n in VN}
    omu_r = sb("omu_r", [128, NDB], F32)
    omu_k = sb("omu_k", [128, NDB], F32)
    cvec["omu_r"] = omu_r
    cvec["omu_k"] = omu_k
    cmask = sb("cmask", [128, 4 * C], BF)
    cirep = sb("cirep", [128, C], BF)
    cgno = sb("cgno", [128, NDB * 16], BF)
    cgnb = sb("cgnb", [16, NDB * 128], BF)
    ccsel = sb("ccsel", [128, 8], F32)
    chsel = sb("chsel", [128, 8], F32)
    whalo = sb("whalo", [128, NDB], F32)
    onescol = sb("onescol", [128, 1], BF)
    onescol32 = sb("onescol32", [128, 1], F32)
    onesrow = sb("onesrow", [1, 128], F32)
    onesrow_b = sb("onesrow_b", [1, 128], BF)
    hsq = sb("hsq", [128, NDB], F32)

    xt = sb("xt", [128, NDB * FT], F32)
    xbf = sb("xbf", [128, NDB * FT], BF)
    ut = sb("ut", [128, NDB * UW], BF)
    bl = [sb(f"bl{i}", [128, NDB * FT], BF) for i in range(2)]
    wbuf = [sb(f"wbuf{i}", [128, NDB * D], BF) for i in range(2)]
    eAd = [sb("eAd0", [128, TQ], BF)]
    spt = sb("spt", [128, TQ], F32)
    gct = sb("gct", [128, TQ], F32)
    gmask = sb("gmask", [128, TQ], BF)
    statsB = sb("statsB", [16, TQ], F32)
    hstat = sb("hstat", [1, 16], F32)
    rstdab = sb("rstdab", [16, 2 * TQ], BF)
    stg = [sb(f"stg{i}", [128, FT], BF) for i in range(4)]
    stg32 = [sb(f"stg32_{i}", [128, FT], F32) for i in range(2)]
    ehcol = sb("ehcol", [128, NDB * (NCH + 1)], F32)
    eecol = sb("eecol", [128, NDB * NCH], F32)

    CW = 8 * C
    scn = ("nt", "mbbr", "mbkr", "atok", "btk", "ktk",
           "aj", "tj", "aj2", "tj2", "ztt", "kt", "u0")
    S3 = {n: sb("p3_" + n, [128, CW], BF) for n in scn}
    x0b = sb("x0b", [128, 2 * CW], BF)
    x1b = sb("x1b", [128, 2 * CW], BF)
    tcb = sb("tcb", [128, CW], BF)
    tcf = sb("tcf", [128, CW], F32)
    xqb = sb("xqb", [128, CW], BF)
    xqf = sb("xqf", [128, CW], F32)
    stw = {n: [sb(f"stw_{n}{i}", [128, NDB * C], BF) for i in range(2)]
           for n in ("a", "b", "k", "r")}
    vw = [sb(f"vw{i}", [128, D], BF) for i in range(2)]
    cwin = {n: [sb(f"cw_{n}{i}", [128, CW], BF) for i in range(2)]
            for n in ("zt", "br", "bt")}
    cp_a = sb("cp_a", [128, 8 * C], BF)
    cp_b = sb("cp_b", [128, 8 * C], BF)
    cp_sin = sb("cp_sin", [128, 8 * C], F32)
    oc = [sb(f"oc{i}", [128, 8 * C], BF) for i in range(2)]

    pj = [psum(f"pj{i}", [128, FT]) for i in range(4)]
    pst = psum("pst", [128, FT])
    pbc = [psum(f"pbc{i}", [128, FT]) for i in range(2)]
    pex = psum("pex", [128, FT])

    sq = sem("sq")
    gq = sem("gq")
    mm = sem("mm")
    ve = sem("ve")
    scs = sem("scs")
    cc = sem("cc")
    oq = sem("oq")

    xv = x_in.rearrange("(b p) t -> p b t", p=128)
    outv = out_t.rearrange("(b p) t -> p b t", p=128)
    x1v = s_x1.rearrange("(b p) t -> p b t", p=128)
    wvw = {n: wt[n].rearrange("(b p) e -> p b e", p=128) for n in WN}

    def T(f):
        P["tensor"].append(f)

    def V(f):
        P["vector"].append(f)

    def S(f):
        P["scalar"].append(f)

    def G(f):
        P["gpsimd"].append(f)

    def Y(f):
        P["sync"].append(f)

    def ap3(t, off, dims):
        return bass.AP(t, off, dims)

    def u_ap(k, a, b):
        return ut[:, k * UW + 1 + a: k * UW + 1 + b]

    def uprev_ap(k, a, b):
        return ut[:, k * UW + a: k * UW + b]

    def fy0(y):
        for n in VN:
            y.dma_start(cvec[n][:, :], vecs[n][:, :]).then_inc(sq.h, 16)
        y.dma_start(ccsel[:, :], csel_in[:, :]).then_inc(sq.h, 16)
        y.dma_start(chsel[:, :], hsel_in[:, :]).then_inc(sq.h, 16)
        y.dma_start(whalo[:, :], xhalo[:, :]).then_inc(sq.h, 16)
    Y(fy0)
    sq.inc(16 * (len(VN) + 3))
    n_setup_sq = sq.n

    def fg0(g):
        g.dma_start(cmask[:, :], masks_in[:, :]).then_inc(gq.h, 16)
        g.dma_start(cirep[:, :], irep_in[:, :]).then_inc(gq.h, 16)
        g.dma_start(cgno[:, :], gnones_in[:, :]).then_inc(gq.h, 16)
        g.dma_start(cgnb[:, :], gnbc_in[:, :]).then_inc(gq.h, 16)
    G(fg0)
    gq.inc(64)
    n_setup_gq = gq.n

    def fv0(v, n=n_setup_sq):
        v.memset(onescol[:, :], 1.0)
        v.memset(onescol32[:, :], 1.0)
        v.memset(onesrow[:, :], 1.0)
        v.memset(onesrow_b[:, :], 1.0)
        v.memset(gmask[:, :], 1.0)
        v.memset(hstat[0:1, 7:8], LN_EPS)
        v.memset(ap3(gmask, 0, [[TQ, 128], [C, NCH]]), 0.0)
        v.wait_ge(sq.h, n)
        v.tensor_scalar(omu_r[:, :], cvec["mu_r"][:, :], -1.0, 1.0,
                        ALU.mult, ALU.add)
        v.tensor_scalar(omu_k[:, :], cvec["mu_k"][:, :], -1.0, 1.0,
                        ALU.mult, ALU.add).then_inc(ve.h, 1)
    V(fv0)
    n_setup_ve = ve.inc(1)

    def load_weight(name, bi, wait_mm=0):
        def f(g, name=name, bi=bi, wait_mm=wait_mm):
            g.wait_ge(mm.h, wait_mm)
            g.dma_start(ap3(wbuf[bi], 0, [[NDB * D, 128], [D, NDB], [1, D]]),
                        wvw[name]).then_inc(gq.h, 16)
        G(f)
        return gq.inc(16)

    n_w0 = load_weight(WN[0], 0)

    n_x32 = []
    n_var = []
    last_u_ve = [n_setup_ve]
    for ft in range(NFT):
        def fy(y, ft=ft):
            y.dma_start(ap3(xt, 0, [[NDB * FT, 128], [FT, NDB], [1, FT]]),
                        xv[:, :, ft * FT:(ft + 1) * FT]).then_inc(sq.h, 16)
        def fgl(g, ft=ft, pmm=mm.n):
            g.wait_ge(mm.h, pmm)
            g.dma_start(ap3(xbf, 0, [[NDB * FT, 128], [FT, NDB], [1, FT]]),
                        xv[:, :, ft * FT:(ft + 1) * FT]).then_inc(gq.h, 16)
        G(fgl)
        n_xb = gq.inc(16)

        prev_sc = scs.n

        def fm(t, n_xb=n_xb, prev=ve.n, prev_sc=prev_sc):
            t.wait_ge(gq.h, n_xb)
            t.wait_ge(ve.h, prev)
            t.wait_ge(scs.h, prev_sc)
            for k in range(NDB):
                mi = t.matmul(pst[0:1, :], onescol[:, :],
                              xbf[:, k * FT:(k + 1) * FT],
                              start=(k == 0), stop=(k == NDB - 1))
            mi.then_inc(mm.h, 1)
        T(fm)
        n_mean = mm.inc(1)

        def fsq(s, n_mean=n_mean):
            s.wait_ge(mm.h, n_mean)
            s.activation(xbf[:, :], xbf[:, :], AF.Square).then_inc(scs.h, 1)
        S(fsq)
        n_sq = scs.inc(1)

        def fms(t, n_sq=n_sq):
            t.wait_ge(scs.h, n_sq)
            for k in range(NDB):
                mi = t.matmul(pst[32:33, :], onescol[:, :],
                              xbf[:, k * FT:(k + 1) * FT],
                              start=(k == 0), stop=(k == NDB - 1))
            mi.then_inc(mm.h, 1)
        T(fms)
        n_msq = mm.inc(1)

        def fst(v, ft=ft, n_msq=n_msq):
            v.wait_ge(mm.h, n_msq)
            sl = slice(ft * FT, (ft + 1) * FT)
            v.tensor_scalar_mul(gct[0:1, sl], pst[0:1, :], 1.0 / D)
            v.tensor_scalar_mul(spt[0:1, 0:FT], pst[32:33, :], 1.0 / D)
            v.tensor_tensor(spt[0:1, FT:2 * FT], gct[0:1, sl],
                            gct[0:1, sl], ALU.mult)
            v.tensor_tensor(statsB[0:1, sl], spt[0:1, 0:FT],
                            spt[0:1, FT:2 * FT], ALU.subtract)
            v.tensor_scalar_add(statsB[0:1, sl], statsB[0:1, sl],
                                LN_EPS).then_inc(ve.h, 1)
        V(fst)
        n_var.append(ve.inc(1))

    def fhm(t, n=n_setup_sq, nv=ve.n):
        t.wait_ge(sq.h, n)
        t.wait_ge(ve.h, nv)
        mi = t.matmul(pst[0:1, 0:NDB], onescol32[:, :], whalo[:, :],
                      start=True, stop=True)
        mi.then_inc(mm.h, 1)
    T(fhm)
    n_hm = mm.inc(1)

    def fhsq(v, n_hm=n_hm):
        v.wait_ge(mm.h, n_hm)
        v.tensor_tensor(hsq[:, :], whalo[:, :], whalo[:, :],
                        ALU.mult).then_inc(ve.h, 1)
    V(fhsq)
    n_hsq = ve.inc(1)

    def fhm2(t, n_hsq=n_hsq):
        t.wait_ge(ve.h, n_hsq)
        mi = t.matmul(pst[32:33, 0:NDB], onescol32[:, :], hsq[:, :],
                      start=True, stop=True)
        mi.then_inc(mm.h, 1)
    T(fhm2)
    n_hm2 = mm.inc(1)

    def fhv(v, n_hm2=n_hm2):
        v.wait_ge(mm.h, n_hm2)
        v.reduce_sum(hstat[0:1, 0:1], pst[0:1, 0:NDB], AX.X)
        vi = v.reduce_sum(hstat[0:1, 3:4], pst[32:33, 0:NDB], AX.X)
        vi.then_inc(ve.h, 1)
    V(fhv)
    n_hv0 = ve.inc(1)

    def fhs2(s, n_hv0=n_hv0):
        s.wait_ge(ve.h, n_hv0)
        s.activation(hstat[0:1, 1:2], hstat[0:1, 0:1], AF.Copy,
                     scale=1.0 / D)
        s.activation(hstat[0:1, 2:3], hstat[0:1, 1:2], AF.Square)
        si = s.activation(hstat[0:1, 3:4], hstat[0:1, 3:4], AF.Copy,
                          scale=1.0 / D)
        si.then_inc(scs.h, 1)
    S(fhs2)
    n_hs2 = scs.inc(1)

    def fhv2(v, n_hs2=n_hs2):
        v.wait_ge(scs.h, n_hs2)
        vi = v.tensor_tensor(hstat[0:1, 3:4], hstat[0:1, 3:4],
                             hstat[0:1, 2:3], ALU.subtract)
        vi.then_inc(ve.h, 1)
    V(fhv2)
    n_hv = ve.inc(1)

    def fsrt(s, n_hv=n_hv):
        s.wait_ge(ve.h, n_hv)
        s.activation(statsB[0:1, :], statsB[0:1, :], AF.Ln)
        s.activation(statsB[0:1, :], statsB[0:1, :], AF.Exp, scale=-0.5)
        s.activation(hstat[0:1, 3:4], hstat[0:1, 3:4], AF.Ln,
                     bias=hstat[0:1, 7:8])
        s.activation(hstat[0:1, 4:5], hstat[0:1, 3:4], AF.Exp,
                     scale=-0.5).then_inc(scs.h, 1)
    S(fsrt)
    n_srt = scs.inc(1)

    def frc(v, n_srt=n_srt):
        v.wait_ge(scs.h, n_srt)
        v.tensor_tensor(gct[0:1, :], gct[0:1, :], statsB[0:1, :], ALU.mult)
        v.tensor_tensor(hstat[0:1, 5:6], hstat[0:1, 1:2],
                        hstat[0:1, 4:5], ALU.mult).then_inc(ve.h, 1)
    V(frc)
    n_rc = ve.inc(1)

    for ft in range(NFT):
        def fbc(t, ft=ft, n_rc=n_rc, pve=ve.n):
            t.wait_ge(ve.h, max(n_rc, pve))
            sl = slice(ft * FT, (ft + 1) * FT)
            t.matmul(pbc[0][:, :], onesrow[:, :], gct[0:1, sl],
                     start=True, stop=True)
            mi = t.matmul(pbc[1][:, :], onesrow[:, :], statsB[0:1, sl],
                          start=True, stop=True)
            mi.then_inc(mm.h, 1)
        T(fbc)
        n_bc = mm.inc(1)

        def fy(y, ft=ft, pve=ve.n):
            y.wait_ge(ve.h, pve)
            y.dma_start(ap3(xt, 0, [[NDB * FT, 128], [FT, NDB], [1, FT]]),
                        xv[:, :, ft * FT:(ft + 1) * FT]).then_inc(sq.h, 16)
        Y(fy)
        n_x = sq.inc(16)

        def fu(v, ft=ft, n_bc=n_bc, n_x=n_x):
            v.wait_ge(mm.h, n_bc)
            v.wait_ge(sq.h, n_x)
            for k in range(NDB):
                v.tensor_tensor(stg32[0][:, :], xt[:, k * FT:(k + 1) * FT],
                                pbc[1][:, :], ALU.mult)
                vi = v.tensor_tensor(u_ap(k, ft * FT, (ft + 1) * FT),
                                     stg32[0][:, :], pbc[0][:, :], ALU.subtract)
            vi.then_inc(ve.h, 1)
        V(fu)
        ve.inc(1)

    def fhb(t, n_rc=n_rc, pve=ve.n):
        t.wait_ge(ve.h, max(n_rc, pve))
        t.matmul(pbc[0][:, 0:1], onesrow[:, :], hstat[0:1, 5:6],
                 start=True, stop=True)
        mi = t.matmul(pbc[1][:, 0:1], onesrow[:, :], hstat[0:1, 4:5],
                      start=True, stop=True)
        mi.then_inc(mm.h, 1)
    T(fhb)
    n_hb = mm.inc(1)

    def fhu(v, n_hb=n_hb):
        v.wait_ge(mm.h, n_hb)
        v.tensor_scalar(hsq[:, :], whalo[:, :], pbc[1][:, 0:1],
                        pbc[0][:, 0:1], ALU.mult, ALU.subtract)
        vi = v.tensor_scalar(ap3(ut, 0, [[NDB * UW, 128], [UW, NDB]]),
                             whalo[:, :], pbc[1][:, 0:1],
                             pbc[0][:, 0:1], ALU.mult, ALU.subtract)
        vi.then_inc(ve.h, 1)
    V(fhu)
    n_uall = ve.inc(1)

    if dev_dump:
        def fdbgh(g, n_uall=n_uall):
            g.wait_ge(ve.h, n_uall)
            g.dma_start(s_dbg[:, 6 * CW:6 * CW + NDB], hsq[:, :]).then_inc(gq.h, 16)
            g.dma_start(s_dbg[:, 6 * CW + 16:6 * CW + 16 + NDB],
                        ap3(ut, 0, [[NDB * UW, 128], [UW, NDB], [1, 1]]),
                        allow_slow_non_contiguous=True).then_inc(gq.h, 16)
            g.dma_start(s_dbg[0:1, 6 * CW + 32:6 * CW + 48],
                        hstat[0:1, 0:16]).then_inc(gq.h, 16)
            g.dma_start(s_dbg[:, 6 * CW + 64:6 * CW + 64 + NDB],
                        whalo[:, :]).then_inc(gq.h, 16)
        G(fdbgh)
        gq.inc(64)


    def blend_tiles(proj_i, mixname, ft, bli, waits_ve, wait_mm=0):
        def fb(v, mixname=mixname, ft=ft, bli=bli, waits_ve=waits_ve,
               wait_mm=wait_mm):
            v.wait_ge(ve.h, waits_ve)
            v.wait_ge(mm.h, wait_mm)
            for k in range(NDB):
                v.tensor_tensor(stg32[1][:, :],
                                uprev_ap(k, ft * FT, (ft + 1) * FT),
                                u_ap(k, ft * FT, (ft + 1) * FT), ALU.subtract)
                vi = v.scalar_tensor_tensor(
                    bl[bli][:, k * FT:(k + 1) * FT], stg32[1][:, :],
                    cvec[mixname][:, k:k + 1],
                    u_ap(k, ft * FT, (ft + 1) * FT), ALU.mult, ALU.add)
            vi.then_inc(ve.h, 1)
        V(fb)
        return ve.inc(1)

    def spill(eng_append, dram, col_off, src, wsem, wn):
        def f(y, dram=dram, col_off=col_off, src=src, wsem=wsem, wn=wn):
            y.wait_ge(wsem.h, wn)
            y.dma_start(dram[:, col_off:col_off + FT], src[:, :]).then_inc(
                sq.h, 16)
        eng_append(f)
        return sq.inc(16)

    n_wnext = load_weight(WN[1], 1)
    bl_last_mm = [0, 0]
    bl_last_ve = [0, 0]
    wbuf_last_mm = [0, 0]
    pjm_ve = [0, 0, 0, 0]
    pjm_scs = [0, 0, 0, 0]

    def std_proj_mms(widx, mixname, evac_after, uw=None):
        name = WN[widx]
        n_w = wsel_cycle[name]
        for ft in range(NFT):
            nb = blend_tiles(widx, mixname, ft, ft % 2,
                             n_uall if uw is None else uw,
                             bl_last_mm[ft % 2])
            for pb in range(8):
                def fmm(t, ft=ft, pb=pb, nb=nb, n_w=n_w,
                        pve=pjm_ve[pb % 4], psc=pjm_scs[pb % 4],
                        widx=widx):
                    t.wait_ge(ve.h, nb)
                    t.wait_ge(gq.h, n_w)
                    t.wait_ge(ve.h, pve)
                    t.wait_ge(scs.h, psc)
                    for cb in range(8):
                        mi = t.matmul(pj[pb % 4][:, :],
                                      wbuf[widx % 2][:, cb * D + pb * 128:
                                                     cb * D + pb * 128 + 128],
                                      bl[ft % 2][:, cb * FT:(cb + 1) * FT],
                                      start=(cb == 0), stop=(cb == 7))
                    mi.then_inc(mm.h, 1)
                T(fmm)
                n_m = mm.inc(1)
                evac_after(ft, pb, n_m)
                pjm_ve[pb % 4] = ve.n
                pjm_scs[pb % 4] = scs.n
            bl_last_mm[ft % 2] = mm.n
            bl_last_ve[ft % 2] = ve.n
            wbuf_last_mm[widx % 2] = mm.n

    wsel_cycle = {WN[0]: n_w0, WN[1]: n_wnext}

    def ev_ww(ft, pb, n_m):
        def fe(s, ft=ft, pb=pb, n_m=n_m, psq=sq.n):
            s.wait_ge(mm.h, n_m)
            s.wait_ge(sq.h, psq)
            s.activation(stg32[0][:, :], pj[pb % 4][:, :], AF.Exp)
            s.activation(stg32[0][:, :], stg32[0][:, :], AF.Ln,
                         bias=1.0).then_inc(scs.h, 1)
        S(fe)
        n_e = scs.inc(1)

        def fsp(y, ft=ft, pb=pb, n_e=n_e):
            y.wait_ge(scs.h, n_e)
            y.dma_start(s_sp[:, pb * TQ + ft * FT:pb * TQ + (ft + 1) * FT],
                        stg32[0][:, :]).then_inc(sq.h, 16)
        Y(fsp)
        sq.inc(16)

    std_proj_mms(0, "mx_w", ev_ww)

    gam_done = []
    for pb in range(8):
        ebuf = 0

        def fld(y, pb=pb, psc=scs.n):
            y.wait_ge(scs.h, psc)
            y.dma_start(spt[:, :], s_sp[:, pb * TQ:(pb + 1) * TQ]
                        ).then_inc(sq.h, 16)
        Y(fld)
        n_l = sq.inc(16)

        def fsc(v, pb=pb, n_l=n_l, psc=scs.n):
            v.wait_ge(sq.h, n_l)
            v.wait_ge(scs.h, psc)
            v.tensor_tensor_scan(gct[:, :], gmask[:, :], spt[:, :], 0.0,
                                 ALU.mult, ALU.add)
            vi = v.tensor_scalar_mul(
                ehcol[:, pb * (NCH + 1):pb * (NCH + 1) + NCH],
                ap3(gct, C - 1, [[TQ, 128], [C, NCH]]), 0.5)
            vi.then_inc(ve.h, 1)
        V(fsc)
        n_gc = ve.inc(1)

        def fsh(v, pb=pb, n_gc=n_gc):
            v.wait_ge(ve.h, n_gc)
            hb = ap3(ehcol, pb * (NCH + 1), [[NDB * (NCH + 1), 128],
                                             [1, NCH], [0, C]])
            g3 = ap3(gct, 0, [[TQ, 128], [C, NCH], [1, C]])
            s3 = ap3(spt, 0, [[TQ, 128], [C, NCH], [1, C]])
            v.tensor_tensor(g3, g3, hb, ALU.subtract)
            vi = v.tensor_tensor(spt[:, :], spt[:, :], gct[:, :], ALU.subtract)
            vi.then_inc(ve.h, 1)
        V(fsh)
        n_sh = ve.inc(1)

        def fex1(s, pb=pb, n_sh=n_sh, pgq=gq.n, pve=ve.n):
            s.wait_ge(ve.h, n_sh)
            s.wait_ge(ve.h, pve)
            s.wait_ge(gq.h, pgq)
            s.activation(eAd[0][:, :], gct[:, :], AF.Exp).then_inc(scs.h, 1)
        S(fex1)
        n_ex1 = scs.inc(1)

        def fspl1(g, pb=pb, n_ex1=n_ex1):
            g.wait_ge(scs.h, n_ex1)
            g.dma_start(s_strip["eBK"][:, pb * TQ:(pb + 1) * TQ],
                        eAd[0][:, :]).then_inc(gq.h, 16)
        G(fspl1)
        n_sp1 = gq.inc(16)

        def fex(s, pb=pb, n_sp1=n_sp1):
            s.wait_ge(gq.h, n_sp1)
            s.activation(eAd[0][:, :], spt[:, :], AF.Exp).then_inc(scs.h, 1)
        S(fex)
        n_ex = scs.inc(1)

        def fexr(s, pb=pb, n_ex=n_ex):
            s.activation(spt[:, :], gct[:, :], AF.Exp,
                         scale=-1.0).then_inc(scs.h, 1)
        S(fexr)
        n_exr = scs.inc(1)

        def fsplr(g, pb=pb, n_exr=n_exr):
            g.wait_ge(scs.h, n_exr)
            g.dma_start(ap3(s_strip["eR"], pb * TQ,
                            [[NDB * TQ, 128], [1, TQ]]).bitcast(BF)
                        if False else s_strip["eR"][:, pb * TQ:(pb + 1) * TQ],
                        spt[:, :]).then_inc(gq.h, 16)
        G(fsplr)
        gq.inc(16)

        def fspl(g, pb=pb, n_ex=n_ex):
            g.wait_ge(scs.h, n_ex)
            g.dma_start(s_strip["eA"][:, pb * TQ:(pb + 1) * TQ],
                        eAd[0][:, :]).then_inc(gq.h, 16)
        G(fspl)
        gq.inc(16)

        def feh(v, pb=pb, n_exr=n_exr):
            v.wait_ge(scs.h, n_exr)
            base = pb * (NCH + 1)
            v.memset(ehcol[:, base + NCH:base + NCH + 1], 0.0)
            vi = v.tensor_tensor(eecol[:, pb * NCH:(pb + 1) * NCH],
                                 ehcol[:, base:base + NCH],
                                 ehcol[:, base + 1:base + NCH + 1], ALU.add)
            vi.then_inc(ve.h, 1)
        V(feh)
        n_feh = ve.inc(1)

        def fehx(s, pb=pb, n_feh=n_feh):
            s.wait_ge(ve.h, n_feh)
            base = pb * (NCH + 1)
            s.activation(ehcol[:, base:base + NCH],
                         ehcol[:, base:base + NCH], AF.Exp, scale=-1.0)
            s.activation(eecol[:, pb * NCH:(pb + 1) * NCH],
                         eecol[:, pb * NCH:(pb + 1) * NCH], AF.Exp,
                         scale=-1.0).then_inc(scs.h, 1)
        S(fehx)
        scs.inc(1)
        gam_done.append(ve.n)
    n_gamma = gam_done[-1]
    n_gamma_sc = scs.n

    def emit_wv():
        name = "W_v"
        n_w = wsel_cycle[name]
        wsel_cycle[WN[2]] = load_weight(WN[2], 0, wbuf_last_mm[0])
        for ft in range(NFT):
            nb = blend_tiles(1, "mx_v", ft, ft % 2, n_uall,
                             bl_last_mm[ft % 2])
            for tb in range(4):
                for eh in range(2):
                    def fmm(t, ft=ft, tb=tb, eh=eh, nb=nb, n_w=n_w,
                            pve=pjm_ve[(tb * 2 + eh) % 4]):
                        t.wait_ge(ve.h, nb)
                        t.wait_ge(gq.h, n_w)
                        t.wait_ge(ve.h, pve)
                        for cb in range(8):
                            mi = t.matmul(
                                pj[(tb * 2 + eh) % 4][:, :],
                                bl[ft % 2][:, cb * FT + tb * 128:
                                           cb * FT + (tb + 1) * 128],
                                wbuf[1][:, cb * D + eh * FT:
                                        cb * D + (eh + 1) * FT],
                                start=(cb == 0), stop=(cb == 7))
                        mi.then_inc(mm.h, 1)
                    T(fmm)
                    n_m = mm.inc(1)

                    def fev(v, ft=ft, tb=tb, eh=eh, n_m=n_m, psq=sq.n):
                        v.wait_ge(mm.h, n_m)
                        v.wait_ge(sq.h, psq)
                        vi = v.tensor_copy(stg[(tb * 2 + eh) % 4][:, :],
                                           pj[(tb * 2 + eh) % 4][:, :])
                        vi.then_inc(ve.h, 1)
                    V(fev)
                    n_e = ve.inc(1)
                    def fsp(y, ft=ft, tb=tb, eh=eh, n_e=n_e):
                        y.wait_ge(ve.h, n_e)
                        gtb = ft * 4 + tb
                        y.dma_start(s_v[:, gtb * D + eh * FT:
                                        gtb * D + (eh + 1) * FT],
                                    stg[(tb * 2 + eh) % 4][:, :]).then_inc(
                            sq.h, 16)
                    Y(fsp)
                    sq.inc(16)
                    pjm_ve[(tb * 2 + eh) % 4] = ve.n
            bl_last_mm[ft % 2] = mm.n
            bl_last_ve[ft % 2] = ve.n
            wbuf_last_mm[1] = mm.n
    emit_wv()

    def emit_wr():
        name = "W_r"
        n_w = wsel_cycle[name]
        wsel_cycle[WN[3]] = load_weight(WN[3], 1, wbuf_last_mm[1])
        for ft in range(NFT):
            nb = blend_tiles(2, "mx_r", ft, ft % 2, n_uall,
                             bl_last_mm[ft % 2])
            for pb in range(8):
                def fld(y, ft=ft, pb=pb, pve=ve.n, pgq=gq.n):
                    y.wait_ge(ve.h, pve)
                    y.wait_ge(gq.h, pgq)
                    y.dma_start(stg[3][:, :],
                                s_strip["eR"][:, pb * TQ + ft * FT:
                                              pb * TQ + (ft + 1) * FT]
                                ).then_inc(sq.h, 16)
                Y(fld)
                n_l = sq.inc(16)

                def fmm(t, ft=ft, pb=pb, nb=nb, n_w=n_w,
                        pve=pjm_ve[pb % 4]):
                    t.wait_ge(ve.h, nb)
                    t.wait_ge(gq.h, n_w)
                    t.wait_ge(ve.h, pve)
                    for cb in range(8):
                        mi = t.matmul(pj[pb % 4][:, :],
                                      wbuf[0][:, cb * D + pb * 128:
                                              cb * D + pb * 128 + 128],
                                      bl[ft % 2][:, cb * FT:(cb + 1) * FT],
                                      start=(cb == 0), stop=(cb == 7))
                    mi.then_inc(mm.h, 1)
                T(fmm)
                n_m = mm.inc(1)

                def fev(v, ft=ft, pb=pb, n_m=n_m, n_l=n_l, psq=sq.n):
                    v.wait_ge(mm.h, n_m)
                    v.wait_ge(sq.h, max(n_l, psq))
                    v.tensor_tensor(stg[pb % 2][:, :], pj[pb % 4][:, :],
                                    stg[3][:, :], ALU.mult).then_inc(ve.h, 1)
                V(fev)
                n_e = ve.inc(1)
                spill(Y, s_strip["rh"], pb * TQ + ft * FT, stg[pb % 2],
                      ve, n_e)
                pjm_ve[pb % 4] = ve.n
            bl_last_mm[ft % 2] = mm.n
            bl_last_ve[ft % 2] = ve.n
            wbuf_last_mm[0] = mm.n
    emit_wr()

    def emit_wk():
        name = "W_k"
        n_w = wsel_cycle[name]
        wsel_cycle[WN[4]] = load_weight(WN[4], 0, wbuf_last_mm[0])
        for ft in range(NFT):
            nb = blend_tiles(3, "mx_k", ft, ft % 2, n_uall,
                             bl_last_mm[ft % 2])
            for pb in range(8):
                def fld(y, ft=ft, pb=pb, pve=ve.n, pgq=gq.n):
                    y.wait_ge(ve.h, pve)
                    y.wait_ge(gq.h, pgq)
                    y.dma_start(stg[3][:, :],
                                s_strip["eBK"][:, pb * TQ + ft * FT:
                                               pb * TQ + (ft + 1) * FT]
                                ).then_inc(sq.h, 16)
                Y(fld)
                n_l = sq.inc(16)

                def fmm(t, ft=ft, pb=pb, nb=nb, n_w=n_w,
                        pscs=pjm_scs[pb % 4], pve=pjm_ve[pb % 4]):
                    t.wait_ge(ve.h, nb)
                    t.wait_ge(gq.h, n_w)
                    t.wait_ge(scs.h, pscs)
                    t.wait_ge(ve.h, pve)
                    for cb in range(8):
                        mi = t.matmul(pj[pb % 4][:, :],
                                      wbuf[1][:, cb * D + pb * 128:
                                              cb * D + pb * 128 + 128],
                                      bl[ft % 2][:, cb * FT:(cb + 1) * FT],
                                      start=(cb == 0), stop=(cb == 7))
                    mi.then_inc(mm.h, 1)
                T(fmm)
                n_m = mm.inc(1)

                def fth(s, ft=ft, pb=pb, n_m=n_m, pve=ve.n):
                    s.wait_ge(mm.h, n_m)
                    s.wait_ge(ve.h, pve)
                    s.activation(stg[2][:, :], pj[pb % 4][:, :],
                                 AF.Tanh).then_inc(scs.h, 1)
                S(fth)
                n_t = scs.inc(1)

                def fev(v, ft=ft, pb=pb, n_t=n_t, n_l=n_l, psq=sq.n):
                    v.wait_ge(scs.h, n_t)
                    v.wait_ge(sq.h, max(n_l, psq))
                    v.tensor_tensor(stg[pb % 2][:, :], stg[2][:, :],
                                    stg[3][:, :], ALU.mult).then_inc(ve.h, 1)
                V(fev)
                n_e = ve.inc(1)
                spill(Y, s_strip["kh"], pb * TQ + ft * FT, stg[pb % 2],
                      ve, n_e)
                pjm_scs[pb % 4] = n_t
                pjm_ve[pb % 4] = ve.n
            bl_last_mm[ft % 2] = mm.n
            bl_last_ve[ft % 2] = ve.n
            wbuf_last_mm[1] = mm.n
    emit_wk()

    def emit_wab(widx, mixi, rawname, rscol):
        name = WN[widx]
        n_w = wsel_cycle[name]
        if widx + 1 < len(WN):
            wsel_cycle[WN[widx + 1]] = load_weight(
                WN[widx + 1], (widx + 1) % 2, wbuf_last_mm[(widx + 1) % 2])
        for ft in range(NFT):
            nb = blend_tiles(widx, mixi, ft, ft % 2, n_uall,
                             bl_last_mm[ft % 2])
            sq_mms = []
            for pb in range(8):
                def fmm(t, ft=ft, pb=pb, nb=nb, n_w=n_w,
                        pve=pjm_ve[pb % 4], psc=pjm_scs[pb % 4]):
                    t.wait_ge(ve.h, nb)
                    t.wait_ge(gq.h, n_w)
                    t.wait_ge(ve.h, pve)
                    t.wait_ge(scs.h, psc)
                    for cb in range(8):
                        mi = t.matmul(pj[pb % 4][:, :],
                                      wbuf[widx % 2][:, cb * D + pb * 128:
                                                     cb * D + pb * 128 + 128],
                                      bl[ft % 2][:, cb * FT:(cb + 1) * FT],
                                      start=(cb == 0), stop=(cb == 7))
                    mi.then_inc(mm.h, 1)
                T(fmm)
                n_m = mm.inc(1)

                def fev(v, ft=ft, pb=pb, n_m=n_m, psq=sq.n):
                    v.wait_ge(mm.h, n_m)
                    v.wait_ge(sq.h, psq)
                    vi = v.tensor_copy(stg[pb % 2][:, :], pj[pb % 4][:, :])
                    vi.then_inc(ve.h, 1)
                V(fev)
                n_e = ve.inc(1)
                spill(Y, s_strip[rawname], pb * TQ + ft * FT, stg[pb % 2],
                      ve, n_e)
                pjm_ve[pb % 4] = n_e

                def fsq2(s, pb=pb, n_e=n_e, pmm=mm.n):
                    s.wait_ge(ve.h, n_e)
                    s.wait_ge(mm.h, pmm)
                    s.activation(stg[2][:, :], stg[pb % 2][:, :],
                                 AF.Square).then_inc(scs.h, 1)
                S(fsq2)
                n_s2 = scs.inc(1)

                def fss(t, ft=ft, pb=pb, n_s2=n_s2):
                    t.wait_ge(scs.h, n_s2)
                    mi = t.matmul(pst[0:16, :],
                                  cgno[:, pb * 16:(pb + 1) * 16],
                                  stg[2][:, :],
                                  start=(pb == 0), stop=(pb == 7))
                    mi.then_inc(mm.h, 1)
                T(fss)
                sq_mms.append(mm.inc(1))

            def fcp(v, ft=ft, n=sq_mms[-1]):
                v.wait_ge(mm.h, n)
                vi = v.tensor_copy(statsB[0:16, ft * FT:(ft + 1) * FT],
                                   pst[0:16, :])
                vi.then_inc(ve.h, 1)
            V(fcp)
            ve.inc(1)
            bl_last_mm[0] = mm.n
            wbuf_last_mm[widx % 2] = mm.n

        def ffin_s(s, pv=ve.n):
            s.wait_ge(ve.h, pv)
            s.activation(statsB[0:16, :], statsB[0:16, :], AF.Ln)
            s.activation(statsB[0:16, :], statsB[0:16, :], AF.Exp,
                         scale=-0.5).then_inc(scs.h, 1)
        S(ffin_s)
        n_f = scs.inc(1)

        def ffin_v(v, rscol=rscol, n_f=n_f):
            v.wait_ge(scs.h, n_f)
            with nc.allow_low_precision(reason="rstd as bf16 matmul operand"):
                vi = v.tensor_scalar_min(rstdab[:, rscol:rscol + TQ],
                                         statsB[0:16, :], 1.0 / NORM_EPS)
            vi.then_inc(ve.h, 1)
        V(ffin_v)
        ve.inc(1)

    emit_wab(4, "mx_a", "araw", 0)
    emit_wab(5, "mx_b", "braw", TQ)

    def emit_hat(rawname, outname, ename, rscol):
        for pb in range(8):
            for ft in range(NFT):
                def fld(y, pb=pb, ft=ft, pv=ve.n, pgq=gq.n):
                    y.wait_ge(ve.h, pv)
                    y.wait_ge(gq.h, pgq)
                    y.dma_start(stg[0][:, :],
                                s_strip[rawname][:, pb * TQ + ft * FT:
                                                 pb * TQ + (ft + 1) * FT]
                                ).then_inc(sq.h, 16)
                    y.dma_start(stg[1][:, :],
                                s_strip[ename][:, pb * TQ + ft * FT:
                                               pb * TQ + (ft + 1) * FT]
                                ).then_inc(sq.h, 16)
                Y(fld)
                n_l = sq.inc(32)

                def fbc(t, pb=pb, ft=ft, pve=ve.n, rscol=rscol):
                    t.wait_ge(ve.h, pve)
                    mi = t.matmul(pbc[0][:, :],
                                  cgnb[:, pb * 128:(pb + 1) * 128],
                                  rstdab[:, rscol + ft * FT:
                                         rscol + (ft + 1) * FT],
                                  start=True, stop=True)
                    mi.then_inc(mm.h, 1)
                T(fbc)
                n_b = mm.inc(1)

                def fap(v, pb=pb, ft=ft, n_b=n_b, n_l=n_l, psq=sq.n):
                    v.wait_ge(mm.h, n_b)
                    v.wait_ge(sq.h, max(n_l, psq))
                    v.tensor_tensor(stg32[0][:, :], stg[0][:, :],
                                    pbc[0][:, :], ALU.mult)
                    vi = v.tensor_tensor(stg[2][:, :], stg32[0][:, :],
                                         stg[1][:, :], ALU.mult)
                    vi.then_inc(ve.h, 1)
                V(fap)
                n_ap = ve.inc(1)
                spill(Y, s_strip[outname], pb * TQ + ft * FT, stg[2],
                      ve, n_ap)

    emit_hat("araw", "ah", "eA", 0)
    emit_hat("braw", "bh", "eBK", TQ)

    def emit_wg():
        name = "W_g"
        n_w = wsel_cycle[name]
        wsel_cycle[WN[7]] = load_weight(WN[7], 1, wbuf_last_mm[1])
        for ft in range(NFT):
            nb = blend_tiles(6, "mx_g", ft, ft % 2, n_uall,
                             bl_last_mm[ft % 2])
            for pb in range(8):
                def fmm(t, ft=ft, pb=pb, nb=nb, n_w=n_w,
                        pscs=pjm_scs[pb % 4], pve=pjm_ve[pb % 4]):
                    t.wait_ge(ve.h, nb)
                    t.wait_ge(gq.h, n_w)
                    t.wait_ge(scs.h, pscs)
                    t.wait_ge(ve.h, pve)
                    for cb in range(8):
                        mi = t.matmul(pj[pb % 4][:, :],
                                      wbuf[0][:, cb * D + pb * 128:
                                              cb * D + pb * 128 + 128],
                                      bl[ft % 2][:, cb * FT:(cb + 1) * FT],
                                      start=(cb == 0), stop=(cb == 7))
                    mi.then_inc(mm.h, 1)
                T(fmm)
                n_m = mm.inc(1)

                def fev(s, ft=ft, pb=pb, n_m=n_m, psq=sq.n):
                    s.wait_ge(mm.h, n_m)
                    s.wait_ge(sq.h, psq)
                    s.activation(stg[pb % 2][:, :], pj[pb % 4][:, :],
                                 AF.Sigmoid).then_inc(scs.h, 1)
                S(fev)
                n_e = scs.inc(1)

                def fsp(y, ft=ft, pb=pb, n_e=n_e):
                    y.wait_ge(scs.h, n_e)
                    y.dma_start(s_strip["g"][:, pb * TQ + ft * FT:
                                             pb * TQ + (ft + 1) * FT],
                                stg[pb % 2][:, :]).then_inc(sq.h, 16)
                Y(fsp)
                sq.inc(16)
                pjm_scs[pb % 4] = n_e
            bl_last_mm[ft % 2] = mm.n
            bl_last_ve[ft % 2] = ve.n
            wbuf_last_mm[0] = mm.n
    emit_wg()

    n_p2_done = dict(sq=sq.n, gq=gq.n, mm=mm.n, ve=ve.n, scs=scs.n)

    def hs(tens, h, w=64, itemw=64):
        r0 = (h % 2) * 64
        c0 = (h // 2) * itemw
        return tens[r0:r0 + 64, c0:c0 + w]

    def sw(name, bi, h):
        r0 = (h % 2) * 64
        c0 = (h // 2) * C
        return stw[name][bi][r0:r0 + 64, c0:c0 + C]

    def pslot(bank, h):
        return bank[(h % 2) * 64:(h % 2) * 64 + 64,
                    (h // 2) * 64:(h // 2) * 64 + 64]

    def mask_ap(blk):
        return ap3(cmask, blk * C, [[4 * C, 128], [0, 8], [1, C]])

    def eh_ap(j):
        return ap3(ehcol, j, [[NDB * (NCH + 1), 128], [NCH + 1, 8], [0, 64]])

    def ee_ap(j):
        return ap3(eecol, j, [[NDB * NCH, 128], [NCH, 8], [0, 64]])

    def irep_ap():
        return ap3(cirep, 0, [[C, 128], [0, 8], [1, C]])

    def fdbge(g, pve=ve.n):
        g.wait_ge(ve.h, n_gamma)
        g.wait_ge(ve.h, pve)
        g.dma_start(s_dbg[:, 4 * CW:4 * CW + NDB * (NCH + 1)],
                    ehcol[:, :]).then_inc(gq.h, 16)
        g.dma_start(s_dbg[:, 5 * CW:5 * CW + NDB * NCH],
                    eecol[:, :]).then_inc(gq.h, 16)
    G(fdbge)
    gq.inc(32)

    def f3init(v, pve=ve.n):
        v.wait_ge(ve.h, n_gamma)
        v.wait_ge(scs.h, n_gamma_sc)
        v.wait_ge(ve.h, pve)
        v.memset(tcf[:, :], 0.0)
        v.memset(tcb[:, :], 0.0)
        v.tensor_tensor(xqf[:, :], irep_ap(), eh_ap(0), ALU.mult)
        vi = v.tensor_copy(xqb[:, :], xqf[:, :])
        vi.then_inc(ve.h, 1)
    V(f3init)
    ve.inc(1)

    last_ev = {"pst": (ve.n, scs.n), "pex": (ve.n, scs.n)}
    pre_banks = [pst, pex]

    strip_loads = []
    p3_spill_gq = []

    def emit_strip_loads(j, pmm_consumed):
        bi = j % 2

        def fl(y, j=j, bi=bi, pmm=pmm_consumed):
            y.wait_ge(mm.h, pmm)
            for nstrip, sname in (("a", "ah"), ("b", "bh"), ("k", "kh"),
                                  ("r", "rh")):
                y.dma_start(
                    ap3(stw[nstrip][bi], 0, [[NDB * C, 128], [C, NDB], [1, C]]),
                    ap3(s_strip[sname], j * C, [[NDB * TQ, 128], [TQ, NDB],
                                                [1, C]])).then_inc(sq.h, 16)
            r0 = (64 * j) % 128
            cb = (j // 2) * D
            y.dma_start(vw[bi][0:64, :],
                        s_v[r0:r0 + 64, cb:cb + D]).then_inc(sq.h, 16)
            y.dma_start(vw[bi][64:128, :],
                        s_v[r0:r0 + 64, cb:cb + D]).then_inc(sq.h, 16)
        Y(fl)
        return sq.inc(16 * 6)

    strip_loads.append(emit_strip_loads(0, 0))
    chunk_mm_done = [0, 0]

    def pre_step(bank_i, stat_f, mov_f, n_ld, extra_mm_wait=0, fj=64):
        bname = "pst" if bank_i == 0 else "pex"
        wv_, ws_ = last_ev[bname]

        def fmm(t, stat_f=stat_f, mov_f=mov_f, n_ld=n_ld, wv_=wv_, ws_=ws_,
                bank_i=bank_i, emw=extra_mm_wait):
            t.wait_ge(sq.h, n_ld)
            t.wait_ge(ve.h, wv_)
            t.wait_ge(scs.h, ws_)
            t.wait_ge(mm.h, emw)
            for h in range(H):
                mi = t.matmul(pslot(pre_banks[bank_i], h), stat_f(h),
                              mov_f(h), start=True, stop=True)
            mi.then_inc(mm.h, 1)
        T(fmm)
        return mm.inc(1)

    def pre_evac_mask(bank_i, dst, blk, n_mm):
        bname = "pst" if bank_i == 0 else "pex"

        def fev(v, dst=dst, blk=blk, n_mm=n_mm, bank_i=bank_i):
            v.wait_ge(mm.h, n_mm)
            d3 = ap3(dst, 0, [[CW, 128], [64, 8], [1, 64]])
            p3_ = ap3(pre_banks[bank_i], 0, [[FT, 128], [64, 8], [1, 64]])
            vi = v.tensor_tensor(d3, p3_, mask_ap(blk), ALU.mult)
            vi.then_inc(ve.h, 1)
        V(fev)
        n = ve.inc(1)
        last_ev[bname] = (n, last_ev[bname][1])
        return n

    def pre_evac_copy(bank_i, dst_ap_f, n_mm):
        bname = "pst" if bank_i == 0 else "pex"

        def fev(s, dst_ap_f=dst_ap_f, n_mm=n_mm, bank_i=bank_i):
            s.wait_ge(mm.h, n_mm)
            s.copy(dst_ap_f(), pre_banks[bank_i][:, :]).then_inc(scs.h, 1)
        S(fev)
        n = scs.inc(1)
        last_ev[bname] = (last_ev[bname][0], n)
        return n

    for j in range(NCH):
        bi = j % 2
        n_ld = strip_loads[j]
        if j + 1 < NCH:
            strip_loads.append(emit_strip_loads(j + 1, chunk_mm_done[(j + 1) % 2]))

        A = lambda h, bi=bi: sw("a", bi, h)
        B = lambda h, bi=bi: sw("b", bi, h)
        Kst = lambda h, bi=bi: sw("k", bi, h)
        R = lambda h, bi=bi: sw("r", bi, h)
        VW = lambda h, bi=bi: vw[bi][(h % 2) * 64:(h % 2) * 64 + 64,
                                     h * 64:h * 64 + 64]
        IR = lambda h: cirep[(h % 2) * 64:(h % 2) * 64 + 64, 0:C]

        n1 = pre_step(0, B, A, n_ld)
        pre_evac_mask(0, S3["tj"], 0, n1)
        n2 = pre_step(1, A, B, n_ld)
        pre_evac_mask(1, S3["aj"], 2, n2)
        n3 = pre_step(0, Kst, A, n_ld)
        pre_evac_mask(0, S3["nt"], 0, n3)
        n4 = pre_step(1, B, R, n_ld)
        def fmbbr_ev(v, n4=n4, pgq=gq.n):
            v.wait_ge(mm.h, n4)
            v.wait_ge(gq.h, pgq)
            d3 = ap3(S3["mbbr"], 0, [[CW, 128], [64, 8], [1, 64]])
            p3_ = ap3(pex, 0, [[FT, 128], [64, 8], [1, 64]])
            vi = v.tensor_tensor(d3, p3_, mask_ap(1), ALU.mult)
            vi.then_inc(ve.h, 1)
        V(fmbbr_ev)
        _n = ve.inc(1)
        last_ev["pex"] = (_n, last_ev["pex"][1])
        n5 = pre_step(0, Kst, R, n_ld)
        pre_evac_mask(0, S3["mbkr"], 1, n5)
        n6 = pre_step(1, A, IR, n_ld)
        pre_evac_copy(1, lambda: S3["atok"][:, :], n6)
        n7 = pre_step(0, B, IR, n_ld)
        pre_evac_copy(0, lambda: S3["btk"][:, :], n7)
        n8 = pre_step(1, Kst, IR, n_ld)
        pre_evac_copy(1, lambda: S3["ktk"][:, :], n8)
        NT = lambda h: hs(S3["nt"], h)
        n9 = pre_step(0, NT, VW, n_ld)
        def fnv(s, n9=n9, j=j, pve=ve.n):
            s.wait_ge(mm.h, n9)
            s.wait_ge(ve.h, pve)
            s.copy(ap3(x0b, 0, [[2 * CW, 128], [128, 8], [1, 64]]),
                   ap3(pst, 0, [[FT, 128], [64, 8], [1, 64]]))
            s.copy(ap3(x0b, 64, [[2 * CW, 128], [128, 8], [1, 64]]),
                   ap3(S3["atok"], 0, [[CW, 128], [64, 8], [1, 64]])
                   ).then_inc(scs.h, 1)
        S(fnv)
        nnv = scs.inc(1)
        last_ev["pst"] = (last_ev["pst"][0], nnv)

        xcur, xnxt = x0b, x1b
        aj_t, tj_t, aj2_t, tj2_t = S3["aj"], S3["tj"], S3["aj2"], S3["tj2"]
        last_tj_t = tj_t
        for lev in range(6):
            TJ = lambda h, t_=tj_t: hs(t_, h)
            XC = lambda h, xcur=xcur: xcur[(h % 2) * 64:(h % 2) * 64 + 64,
                                           (h // 2) * 128:(h // 2) * 128 + 128]

            def fap(t, TJ=TJ, XC=XC, pve=ve.n, psc=scs.n):
                t.wait_ge(ve.h, pve)
                t.wait_ge(scs.h, psc)
                for h in range(H):
                    bank = pst if h < 8 else pex
                    hh = h if h < 8 else h - 8
                    mi = t.matmul(
                        bank[(h % 2) * 64:(h % 2) * 64 + 64,
                             (hh // 2) * 128:(hh // 2) * 128 + 128],
                        TJ(h), XC(h), start=True, stop=True)
                mi.then_inc(mm.h, 1)
            T(fap)
            nap = mm.inc(1)

            def fapev(v, xcur=xcur, xnxt=xnxt, nap=nap):
                v.wait_ge(mm.h, nap)
                v.tensor_tensor(xnxt[:, 0:CW], pst[:, :], xcur[:, 0:CW],
                                ALU.add)
                vi = v.tensor_tensor(xnxt[:, CW:2 * CW], pex[:, :],
                                     xcur[:, CW:2 * CW], ALU.add)
                vi.then_inc(ve.h, 1)
            V(fapev)
            nev = ve.inc(1)
            last_ev["pst"] = (nev, last_ev["pst"][1])
            last_ev["pex"] = (nev, last_ev["pex"][1])
            xcur, xnxt = xnxt, xcur

            if lev < 5:
                AJ = lambda h, t_=aj_t: hs(t_, h)
                TJc = lambda h, t_=tj_t: hs(t_, h)
                na = pre_step(0, TJc, AJ, n_ld)
                pre_evac_copy(0, lambda t_=aj2_t: t_[:, :], na)
                nt_ = pre_step(1, AJ, TJc, n_ld)
                pre_evac_copy(1, lambda t_=tj2_t: t_[:, :], nt_)
                aj_t, aj2_t = aj2_t, aj_t
                tj_t, tj2_t = tj2_t, tj_t

        xfin = xcur

        ZSL = lambda h, xfin=xfin: xfin[(h % 2) * 64:(h % 2) * 64 + 64,
                                        (h // 2) * 128 + 64:
                                        (h // 2) * 128 + 128]
        n10 = pre_step(0, ZSL, IR, n_ld)
        def fztt_ev(s, n10=n10, pgq=gq.n):
            s.wait_ge(mm.h, n10)
            s.wait_ge(gq.h, pgq)
            s.copy(S3["ztt"][:, :], pst[:, :]).then_inc(scs.h, 1)
        S(fztt_ev)
        _n = scs.inc(1)
        last_ev["pst"] = (last_ev["pst"][0], _n)
        BTK = lambda h: hs(S3["btk"], h)
        n11 = pre_step(1, ZSL, BTK, n_ld)
        pre_evac_copy(1, lambda: S3["kt"][:, :], n11)

        wv0, ws0 = last_ev["pst"]
        wv1, ws1 = last_ev["pex"]

        def fsp3(g, j=j, ws0=ws0, wv1=wv1, pve=ve.n, psc=scs.n):
            g.wait_ge(scs.h, psc)
            g.wait_ge(ve.h, pve)
            g.dma_start(s_zt[:, j * CW:(j + 1) * CW],
                        S3["ztt"][:, :]).then_inc(gq.h, 16)
            g.dma_start(s_mbBR[:, j * CW:(j + 1) * CW],
                        S3["mbbr"][:, :]).then_inc(gq.h, 16)
            g.dma_start(s_btok[:, j * CW:(j + 1) * CW],
                        S3["btk"][:, :]).then_inc(gq.h, 16)
        G(fsp3)
        n_sp3 = gq.inc(48)

        ZTT = lambda h: hs(S3["ztt"], h)
        TCB = lambda h: hs(tcb, h)
        U0 = lambda h: hs(S3["u0"], h)
        MBBR = lambda h: hs(S3["mbbr"], h)
        MBKR = lambda h: hs(S3["mbkr"], h)
        KTK = lambda h: hs(S3["ktk"], h)
        KT = lambda h: hs(S3["kt"], h)
        XQ = lambda h: hs(xqb, h)

        wv0, ws0 = last_ev["pst"]

        def fu0(t, pve=ve.n, psc=scs.n, ZTT=ZTT, TCB=TCB):
            t.wait_ge(ve.h, pve)
            t.wait_ge(scs.h, psc)
            for h in range(H):
                mi = t.matmul(pslot(pj[0], h), ZTT(h), TCB(h),
                              start=True, stop=True)
            mi.then_inc(mm.h, 1)
        T(fu0)
        nu0 = mm.inc(1)

        def fu0ev(v, nu0=nu0, xfin=xfin):
            v.wait_ge(mm.h, nu0)
            vi = v.tensor_tensor(
                ap3(S3["u0"], 0, [[CW, 128], [64, 8], [1, 64]]),
                ap3(pj[0], 0, [[FT, 128], [64, 8], [1, 64]]),
                ap3(xfin, 0, [[2 * CW, 128], [128, 8], [1, 64]]), ALU.add)
            vi.then_inc(ve.h, 1)
        V(fu0ev)
        nu0e = ve.inc(1)

        def fo(t, nu0e=nu0e, TCB=TCB, R=R, U0=U0, MBBR=MBBR, VW=VW,
               MBKR=MBKR, pve=ve.n):
            t.wait_ge(ve.h, max(nu0e, pve))
            for h in range(H):
                t.matmul(pslot(pj[1], h), TCB(h), R(h), start=True, stop=False)
                t.matmul(pslot(pj[1], h), U0(h), MBBR(h), start=False,
                         stop=False)
                mi = t.matmul(pslot(pj[1], h), VW(h), MBKR(h), start=False,
                              stop=True)
            mi.then_inc(mm.h, 1)
        T(fo)
        no = mm.inc(1)

        def foev(s, no=no, j=j, pgq=gq.n):
            s.wait_ge(mm.h, no)
            s.wait_ge(gq.h, pgq)
            s.copy(oc[j % 2][:, :], pj[1][:, :]).then_inc(scs.h, 1)
        S(foev)
        n_oev = scs.inc(1)

        def fosp(g, j=j, n_oev=n_oev):
            g.wait_ge(scs.h, n_oev)
            g.dma_start(ap3(s_o, j * C, [[NDB * TQ, 128], [TQ, 8], [1, C]]),
                        oc[j % 2][:, :]).then_inc(gq.h, 16)
            g.dma_start(ap3(s_o0, j * C, [[NDB * TQ, 128], [TQ, 8], [1, C]]),
                        oc[j % 2][:, :]).then_inc(gq.h, 16)
        G(fosp)
        gq.inc(32)

        def fst3(t, nu0e=nu0e, BTK=BTK, U0=U0, KTK=KTK, VW=VW, pve=ve.n):
            t.wait_ge(ve.h, max(nu0e, pve))
            for h in range(H):
                t.matmul(pslot(pj[2], h), BTK(h), U0(h), start=True, stop=False)
                mi = t.matmul(pslot(pj[2], h), KTK(h), VW(h), start=False,
                              stop=True)
            mi.then_inc(mm.h, 1)
        T(fst3)
        nst = mm.inc(1)

        def fstev(v, nst=nst, j=j):
            v.wait_ge(mm.h, nst)
            v.tensor_tensor(tcf[:, :], pj[2][:, :], tcf[:, :], ALU.add)
            vi = v.tensor_tensor(tcf[:, :], tcf[:, :], ee_ap(j), ALU.mult)
            vi.then_inc(ve.h, 1)
        V(fstev)
        nstev = ve.inc(1)

        def fstc(s, nstev=nstev):
            s.wait_ge(ve.h, nstev)
            s.copy(tcb[:, :], tcf[:, :]).then_inc(scs.h, 1)
        S(fstc)
        scs.inc(1)

        def fkx(t, KT=KT, XQ=XQ, pve=ve.n, psc=scs.n):
            t.wait_ge(ve.h, pve)
            t.wait_ge(scs.h, psc)
            for h in range(H):
                mi = t.matmul(pslot(pj[3], h), KT(h), XQ(h), start=True,
                              stop=True)
            mi.then_inc(mm.h, 1)
        T(fkx)
        nkx = mm.inc(1)

        def fkxev(v, nkx=nkx, j=j):
            v.wait_ge(mm.h, nkx)
            v.tensor_tensor(xqf[:, :], pj[3][:, :], xqf[:, :], ALU.add)
            vi = v.tensor_tensor(xqf[:, :], xqf[:, :], ee_ap(j), ALU.mult)
            vi.then_inc(ve.h, 1)
        V(fkxev)
        nkxe = ve.inc(1)

        def fkxc(s, nkxe=nkxe):
            s.wait_ge(ve.h, nkxe)
            s.copy(xqb[:, :], xqf[:, :]).then_inc(scs.h, 1)
        S(fkxc)
        scs.inc(1)

        if j in (8, 10, 12, 14):
            slot_d = (j - 8) // 2
            n_fstc = scs.n

            def fdbgt(g, slot_d=slot_d, n_fstc=n_fstc):
                g.wait_ge(scs.h, n_fstc)
                g.dma_start(s_dbg[:, slot_d * CW:(slot_d + 1) * CW],
                            tcb[:, :]).then_inc(gq.h, 16)
            G(fdbgt)
            n_dump = gq.inc(16)

            def fbar(t, n_dump=n_dump):
                t.wait_ge(gq.h, n_dump)
            T(fbar)

        chunk_mm_done[bi] = mm.n

    def fsum(s, pve=ve.n):
        s.wait_ge(ve.h, pve)
        s.copy(cp_a[:, :], xqf[:, :])
        s.copy(cp_b[:, :], tcf[:, :]).then_inc(scs.h, 1)
    S(fsum)
    n_sum = scs.inc(1)

    def fdbg1(g, pve=ve.n, psc=scs.n):
        g.wait_ge(ve.h, pve)
        g.wait_ge(scs.h, psc)
        g.dma_start(s_dbg[:, 0:CW], tcf[:, :]).then_inc(gq.h, 16)
        g.dma_start(s_dbg[:, CW:2 * CW], xqf[:, :]).then_inc(gq.h, 16)
    G(fdbg1)
    gq.inc(32)

    n_gq_pre = gq.n

    def fsumd(g, n_sum=n_sum, n_gq_pre=n_gq_pre):
        g.wait_ge(scs.h, n_sum)
        g.dma_start(sum_loc[:, 0:CW], cp_a[:, :]).then_inc(gq.h, 16)
        g.dma_start(sum_loc[:, CW:2 * CW], cp_b[:, :]).then_inc(gq.h, 16)
        g.wait_ge(gq.h, n_gq_pre + 32)
        g.collective_compute(
            "AllGather", ALU.bypass, replica_groups=[list(range(8))],
            ins=[sum_loc.ap().opt()], outs=[sum_all.ap().opt()],
        ).then_inc(cc.h, 1)
    G(fsumd)
    gq.inc(32)
    cc.inc(1)

    cp_acc = cp_sin

    def fcpi(v, pcc=cc.n):
        v.wait_ge(cc.h, pcc)
        v.memset(tcf[:, :], 0.0)
        vi = v.memset(cp_sin[:, :], 0.0)
        vi.then_inc(ve.h, 1)
    V(fcpi)
    ve.inc(1)

    for chain_c in range(2):
        def fz(v, pve=ve.n):
            v.wait_ge(ve.h, pve)
            vi = v.memset(cp_sin[:, :], 0.0)
            vi.then_inc(ve.h, 1)
        V(fz)
        ve.inc(1)
        for p in range(4):
            cidx = chain_c * 4 + p
            def fsel(v, cidx=cidx, pve=ve.n):
                v.wait_ge(ve.h, pve)
                vi = v.scalar_tensor_tensor(tcf[:, :], cp_sin[:, :],
                                            ccsel[:, cidx:cidx + 1],
                                            tcf[:, :], ALU.mult, ALU.add)
                vi.then_inc(ve.h, 1)
            V(fsel)
            ve.inc(1)
            if p == 3:
                continue
            def fcl(y, cidx=cidx, pmm=mm.n, pve=ve.n):
                y.wait_ge(mm.h, pmm)
                y.wait_ge(ve.h, pve)
                y.dma_start(cp_a[:, :],
                            sum_all[cidx * 128:(cidx + 1) * 128, 0:CW]
                            ).then_inc(sq.h, 16)
                y.dma_start(cp_b[:, :],
                            sum_all[cidx * 128:(cidx + 1) * 128, CW:2 * CW]
                            ).then_inc(sq.h, 16)
            Y(fcl)
            n_cl = sq.inc(32)

            def fsb(s, pve=ve.n, n_cl=n_cl):
                s.wait_ge(ve.h, pve)
                s.copy(xqb[:, :], cp_sin[:, :]).then_inc(scs.h, 1)
            S(fsb)
            n_sb = scs.inc(1)

            def fq1(t, n_cl=n_cl, n_sb=n_sb, pve=ve.n, psc=scs.n):
                t.wait_ge(sq.h, n_cl)
                t.wait_ge(scs.h, max(n_sb, psc))
                t.wait_ge(ve.h, pve)
                for h in range(H):
                    mi = t.matmul(pslot(pst, h), hs(cp_a, h),
                                  cirep[(h % 2) * 64:(h % 2) * 64 + 64, 0:C],
                                  start=True, stop=True)
                mi.then_inc(mm.h, 1)
            T(fq1)
            n_q1 = mm.inc(1)

            def fq1e(s, n_q1=n_q1):
                s.wait_ge(mm.h, n_q1)
                s.copy(S3["kt"][:, :], pst[:, :]).then_inc(scs.h, 1)
            S(fq1e)
            n_q1e = scs.inc(1)

            def fq2(t, n_q1e=n_q1e):
                t.wait_ge(scs.h, n_q1e)
                for h in range(H):
                    mi = t.matmul(pslot(pex, h), hs(S3["kt"], h), hs(xqb, h),
                                  start=True, stop=True)
                mi.then_inc(mm.h, 1)
            T(fq2)
            n_q2 = mm.inc(1)

            def fq2e(v, n_q2=n_q2):
                v.wait_ge(mm.h, n_q2)
                vi = v.tensor_tensor(cp_sin[:, :], pex[:, :], cp_b[:, :],
                                     ALU.add)
                vi.then_inc(ve.h, 1)
            V(fq2e)
            ve.inc(1)

    def fdbg2(g, pve=ve.n):
        g.wait_ge(ve.h, pve)
        g.dma_start(s_dbg[:, 2 * CW:3 * CW], tcf[:, :]).then_inc(gq.h, 16)
    G(fdbg2)
    gq.inc(16)

    def f3ci(v, pve=ve.n, pgq=gq.n):
        v.wait_ge(ve.h, pve)
        v.wait_ge(gq.h, pgq)
        v.tensor_tensor(tcf[:, :], tcf[:, :], eh_ap(0), ALU.mult)
        vi = v.tensor_copy(tcb[:, :], tcf[:, :])
        vi.then_inc(ve.h, 1)
    V(f3ci)
    ve.inc(1)

    cw_loads = []

    def emit_cw_loads(j, pmm):
        bi = j % 2

        def fl(y, j=j, bi=bi, pmm=pmm, pgq=gq.n):
            y.wait_ge(mm.h, pmm)
            y.wait_ge(gq.h, pgq)
            y.dma_start(cwin["zt"][bi][:, :],
                        s_zt[:, j * CW:(j + 1) * CW]).then_inc(sq.h, 16)
            y.dma_start(cwin["br"][bi][:, :],
                        s_mbBR[:, j * CW:(j + 1) * CW]).then_inc(sq.h, 16)
            y.dma_start(cwin["bt"][bi][:, :],
                        s_btok[:, j * CW:(j + 1) * CW]).then_inc(sq.h, 16)
            y.dma_start(
                ap3(stw["r"][bi], 0, [[NDB * C, 128], [C, NDB], [1, C]]),
                ap3(s_strip["rh"], j * C, [[NDB * TQ, 128], [TQ, NDB],
                                           [1, C]])).then_inc(sq.h, 16)
            y.dma_start(oc[bi][:, :],
                        ap3(s_o, j * C, [[NDB * TQ, 128], [TQ, 8], [1, C]])
                        ).then_inc(sq.h, 16)
        Y(fl)
        return sq.inc(80)

    n_spill_all = gq.n
    cw_loads.append(emit_cw_loads(0, mm.n))
    cw_mm_done = [0, 0]
    for j in range(NCH):
        bi = j % 2
        n_ld = cw_loads[j]
        if j + 1 < NCH:
            cw_loads.append(emit_cw_loads(j + 1, cw_mm_done[(j + 1) % 2]))
        CZT = lambda h, bi=bi: hs(cwin["zt"][bi], h)
        CBR = lambda h, bi=bi: hs(cwin["br"][bi], h)
        CBT = lambda h, bi=bi: hs(cwin["bt"][bi], h)
        R3 = lambda h, bi=bi: sw("r", bi, h)
        TCB = lambda h: hs(tcb, h)
        U0 = lambda h: hs(S3["u0"], h)

        def fdu(t, n_ld=n_ld, CZT=CZT, TCB=TCB, pve=ve.n, psc=scs.n):
            t.wait_ge(sq.h, n_ld)
            t.wait_ge(ve.h, pve)
            t.wait_ge(scs.h, psc)
            for h in range(H):
                mi = t.matmul(pslot(pj[0], h), CZT(h), TCB(h), start=True,
                              stop=True)
            mi.then_inc(mm.h, 1)
        T(fdu)
        ndu = mm.inc(1)

        def fduev(s, ndu=ndu):
            s.wait_ge(mm.h, ndu)
            s.copy(S3["u0"][:, :], pj[0][:, :]).then_inc(scs.h, 1)
        S(fduev)
        ndue = scs.inc(1)

        def fo3(t, ndue=ndue, TCB=TCB, R3=R3, U0=U0, CBR=CBR, pve=ve.n):
            t.wait_ge(scs.h, ndue)
            t.wait_ge(ve.h, pve)
            for h in range(H):
                t.matmul(pslot(pj[1], h), TCB(h), R3(h), start=True,
                         stop=False)
                mi = t.matmul(pslot(pj[1], h), U0(h), CBR(h), start=False,
                              stop=True)
            mi.then_inc(mm.h, 1)
        T(fo3)
        no3 = mm.inc(1)

        def fo3ev(v, no3=no3, j=j, pgq=gq.n):
            v.wait_ge(mm.h, no3)
            v.wait_ge(gq.h, pgq)
            vi = v.tensor_tensor(oc[j % 2][:, :], pj[1][:, :],
                                 oc[j % 2][:, :], ALU.add)
            vi.then_inc(ve.h, 1)
        V(fo3ev)
        n_o3e = ve.inc(1)

        def fo3sp(g, j=j, n_o3e=n_o3e):
            g.wait_ge(ve.h, n_o3e)
            g.dma_start(ap3(s_o, j * C, [[NDB * TQ, 128], [TQ, 8], [1, C]]),
                        oc[j % 2][:, :]).then_inc(gq.h, 16)
        G(fo3sp)
        gq.inc(16)

        def fs3(t, ndue=ndue, CBT=CBT, U0=U0, pve=ve.n):
            t.wait_ge(scs.h, ndue)
            t.wait_ge(ve.h, pve)
            for h in range(H):
                mi = t.matmul(pslot(pj[2], h), CBT(h), U0(h), start=True,
                              stop=True)
            mi.then_inc(mm.h, 1)
        T(fs3)
        ns3 = mm.inc(1)

        def fs3ev(v, ns3=ns3, j=j):
            v.wait_ge(mm.h, ns3)
            v.tensor_tensor(tcf[:, :], pj[2][:, :], tcf[:, :], ALU.add)
            vi = v.tensor_tensor(tcf[:, :], tcf[:, :], ee_ap(j), ALU.mult)
            vi.then_inc(ve.h, 1)
        V(fs3ev)
        ns3e = ve.inc(1)

        def fs3c(s, ns3e=ns3e):
            s.wait_ge(ve.h, ns3e)
            s.copy(tcb[:, :], tcf[:, :]).then_inc(scs.h, 1)
        S(fs3c)
        scs.inc(1)
        cw_mm_done[bi] = mm.n

    wsel_cycle[WN[8]] = load_weight(WN[8], 0, wbuf_last_mm[0])
    GNS = 1.0 / 64.0
    for _b in range(4):
        pjm_ve[_b] = ve.n
        pjm_scs[_b] = scs.n

    for ft in range(NFT):
        def fol(y, ft=ft, pgq=gq.n, bmm=bl_last_mm[ft % 2],
                bve=bl_last_ve[ft % 2]):
            y.wait_ge(gq.h, pgq)
            y.wait_ge(mm.h, bmm)
            y.wait_ge(ve.h, bve)
            for pb in range(8):
                y.dma_start(bl[ft % 2][:, pb * FT:(pb + 1) * FT],
                            s_o[:, pb * TQ + ft * FT:pb * TQ + (ft + 1) * FT]
                            ).then_inc(sq.h, 16)
        Y(fol)
        n_ol = sq.inc(8 * 16)

        def fgs(t, ft=ft, n_ol=n_ol, pve=ve.n):
            t.wait_ge(sq.h, n_ol)
            t.wait_ge(ve.h, pve)
            for pb in range(8):
                mi = t.matmul(pst[0:16, :], cgno[:, pb * 16:(pb + 1) * 16],
                              bl[ft % 2][:, pb * FT:(pb + 1) * FT],
                              start=(pb == 0), stop=(pb == 7))
            mi.then_inc(mm.h, 1)
        T(fgs)
        mm.inc(1)

        n_qs = 0
        for pb in range(8):
            def fsqr(s, ft=ft, pb=pb, n_ol=n_ol, pmm=mm.n):
                s.wait_ge(sq.h, n_ol)
                s.wait_ge(mm.h, pmm)
                s.activation(stg[2 + (pb % 2)][:, :],
                             bl[ft % 2][:, pb * FT:(pb + 1) * FT],
                             AF.Square).then_inc(scs.h, 1)
            S(fsqr)
            n_sq2 = scs.inc(1)

            def fqs(t, pb=pb, n_sq2=n_sq2):
                t.wait_ge(scs.h, n_sq2)
                mi = t.matmul(pex[0:16, :], cgno[:, pb * 16:(pb + 1) * 16],
                              stg[2 + (pb % 2)][:, :],
                              start=(pb == 0), stop=(pb == 7))
                mi.then_inc(mm.h, 1)
            T(fqs)
            n_qs = mm.inc(1)

        def frows(v, n_qs=n_qs, ft=ft):
            v.wait_ge(mm.h, n_qs)
            sl = slice(ft * FT, (ft + 1) * FT)
            with nc.allow_low_precision(reason="gn mean bf16"):
                v.tensor_scalar_mul(rstdab[:, sl], pst[0:16, :], GNS)
            v.tensor_scalar_mul(spt[32:48, 0:FT], pex[0:16, :], GNS)
            v.tensor_tensor(spt[32:48, FT:2 * FT], rstdab[:, sl],
                            rstdab[:, sl], ALU.mult)
            v.tensor_scalar(spt[32:48, FT:2 * FT], spt[32:48, FT:2 * FT],
                            1.0, GN_EPS, ALU.mult, ALU.subtract)
            vi = v.tensor_tensor(spt[32:48, 0:FT], spt[32:48, 0:FT],
                                 spt[32:48, FT:2 * FT], ALU.subtract)
            vi.then_inc(ve.h, 1)
        V(frows)
        n_rows = ve.inc(1)

        def frstd(s, n_rows=n_rows, ft=ft):
            s.wait_ge(ve.h, n_rows)
            sl = slice(TQ + ft * FT, TQ + (ft + 1) * FT)
            s.activation(spt[32:48, 0:FT], spt[32:48, 0:FT], AF.Ln)
            with nc.allow_low_precision(reason="gn rstd bf16"):
                si = s.activation(rstdab[0:16, sl], spt[32:48, 0:FT], AF.Exp,
                                  scale=-0.5)
            si.then_inc(scs.h, 1)
        S(frstd)
        n_rstd = scs.inc(1)

        for pb in range(8):
            def fgl2(y, ft=ft, pb=pb, pve=ve.n):
                y.wait_ge(ve.h, pve)
                y.dma_start(stg[pb % 2][:, :],
                            s_strip["g"][:, pb * TQ + ft * FT:
                                         pb * TQ + (ft + 1) * FT]
                            ).then_inc(sq.h, 16)
            Y(fgl2)
            n_gl = sq.inc(16)

            def fbc2(t, ft=ft, pb=pb, n_rstd=n_rstd, pve=ve.n):
                t.wait_ge(scs.h, n_rstd)
                t.wait_ge(ve.h, pve)
                t.matmul(pbc[0][:, :], cgnb[:, pb * 128:(pb + 1) * 128],
                         rstdab[:, ft * FT:(ft + 1) * FT],
                         start=True, stop=True)
                mi = t.matmul(pbc[1][:, :], cgnb[:, pb * 128:(pb + 1) * 128],
                              rstdab[:, TQ + ft * FT:TQ + (ft + 1) * FT],
                              start=True, stop=True)
                mi.then_inc(mm.h, 1)
            T(fbc2)
            n_b2 = mm.inc(1)

            def fgate(v, ft=ft, pb=pb, n_b2=n_b2, n_gl=n_gl, psq=sq.n):
                v.wait_ge(mm.h, n_b2)
                v.wait_ge(sq.h, max(n_gl, psq))
                c0 = pb * FT
                v.tensor_tensor(stg32[pb % 2][:, :],
                                bl[ft % 2][:, c0:c0 + FT],
                                pbc[0][:, :], ALU.subtract)
                v.tensor_tensor(stg32[pb % 2][:, :], stg32[pb % 2][:, :],
                                pbc[1][:, :], ALU.mult)
                v.tensor_scalar(stg32[pb % 2][:, :], stg32[pb % 2][:, :],
                                cvec["gn_w"][:, pb:pb + 1],
                                cvec["gn_b"][:, pb:pb + 1],
                                ALU.mult, ALU.add)
                with nc.allow_low_precision(reason="gated bf16 operand"):
                    vi = v.tensor_tensor(bl[ft % 2][:, c0:c0 + FT],
                                         stg32[pb % 2][:, :],
                                         stg[pb % 2][:, :], ALU.mult)
                vi.then_inc(ve.h, 1)
            V(fgate)
            ve.inc(1)

        def fxl(y, ft=ft, pve=ve.n, psq=sq.n, pgq=gq.n):
            y.wait_ge(ve.h, pve)
            y.wait_ge(gq.h, pgq)
            y.dma_start(ap3(xt, 0, [[NDB * FT, 128], [FT, NDB], [1, FT]]),
                        xv[:, :, ft * FT:(ft + 1) * FT]).then_inc(sq.h, 16)
        Y(fxl)
        n_xl = sq.inc(16)

        gate_done = ve.n
        for pb in range(8):
            def fwo(t, ft=ft, pb=pb, gd=gate_done, pscs=scs.n,
                    pve=pjm_ve[pb % 4]):
                t.wait_ge(ve.h, max(gd, pve))
                t.wait_ge(scs.h, pscs)
                for cb in range(8):
                    mi = t.matmul(pj[pb % 4][:, :],
                                  wbuf[1][:, cb * D + pb * 128:
                                          cb * D + pb * 128 + 128],
                                  bl[ft % 2][:, cb * FT:(cb + 1) * FT],
                                  start=(cb == 0), stop=(cb == 7))
                mi.then_inc(mm.h, 1)
            T(fwo)
            n_wo = mm.inc(1)

            def fx1(v, ft=ft, pb=pb, n_wo=n_wo, n_xl=n_xl, pgq=gq.n):
                v.wait_ge(mm.h, n_wo)
                v.wait_ge(sq.h, n_xl)
                v.wait_ge(gq.h, pgq)
                c0 = pb * FT
                v.tensor_tensor(xt[:, c0:c0 + FT], pj[pb % 4][:, :],
                                xt[:, c0:c0 + FT], ALU.add)
                v.tensor_scalar(xt[:, c0:c0 + FT], xt[:, c0:c0 + FT],
                                cvec["b_o"][:, pb:pb + 1], 1.0,
                                ALU.add, ALU.mult)
                with nc.allow_low_precision(reason="x1 bf16 stats copy"):
                    vi = v.tensor_copy(xbf[:, c0:c0 + FT], xt[:, c0:c0 + FT])
                vi.then_inc(ve.h, 1)
            V(fx1)
            n_x1 = ve.inc(1)

            def fx1sp(g, ft=ft, pb=pb, n_x1=n_x1):
                g.wait_ge(ve.h, n_x1)
                g.dma_start(s_x1[pb * 128:(pb + 1) * 128,
                                 ft * FT:(ft + 1) * FT],
                            xt[:, pb * FT:(pb + 1) * FT]).then_inc(gq.h, 16)
            G(fx1sp)
            gq.inc(16)
            pjm_ve[pb % 4] = n_x1

        def fl2m(t, pve=ve.n, plnv=pjm_ve[2]):
            t.wait_ge(ve.h, pve)
            t.wait_ge(ve.h, plnv)
            for k in range(NDB):
                mi = t.matmul(pj[2][0:1, :], onescol[:, :],
                              xbf[:, k * FT:(k + 1) * FT],
                              start=(k == 0), stop=(k == NDB - 1))
            mi.then_inc(mm.h, 1)
        T(fl2m)
        n_l2m = mm.inc(1)

        def fl2sq(s, n_l2m=n_l2m):
            s.wait_ge(mm.h, n_l2m)
            s.activation(xbf[:, :], xbf[:, :], AF.Square).then_inc(scs.h, 1)
        S(fl2sq)
        n_l2sq = scs.inc(1)

        def fl2q(t, n_l2sq=n_l2sq, plnv=pjm_ve[3]):
            t.wait_ge(scs.h, n_l2sq)
            t.wait_ge(ve.h, plnv)
            for k in range(NDB):
                mi = t.matmul(pj[3][0:1, :], onescol[:, :],
                              xbf[:, k * FT:(k + 1) * FT],
                              start=(k == 0), stop=(k == NDB - 1))
            mi.then_inc(mm.h, 1)
        T(fl2q)
        n_l2q = mm.inc(1)

        def fl2st(v, ft=ft, n_l2q=n_l2q):
            v.wait_ge(mm.h, n_l2q)
            sl = slice(ft * FT, (ft + 1) * FT)
            v.tensor_scalar_mul(gct[0:1, sl], pj[2][0:1, :], 1.0 / D)
            v.tensor_scalar_mul(spt[0:1, 0:FT], pj[3][0:1, :], 1.0 / D)
            v.tensor_tensor(spt[0:1, FT:2 * FT], gct[0:1, sl],
                            gct[0:1, sl], ALU.mult)
            v.tensor_tensor(statsB[0:1, sl], spt[0:1, 0:FT],
                            spt[0:1, FT:2 * FT], ALU.subtract)
            vi = v.tensor_scalar_add(statsB[0:1, sl], statsB[0:1, sl],
                                     LN_EPS)
            vi.then_inc(ve.h, 1)
        V(fl2st)
        n_l2st = ve.inc(1)
        pjm_ve[2] = n_l2st
        pjm_ve[3] = n_l2st
        bl_last_mm[ft % 2] = mm.n
        bl_last_ve[ft % 2] = ve.n
        wbuf_last_mm[1] = mm.n

    wsel_cycle[WN[9]] = load_weight(WN[9], 1, wbuf_last_mm[1])

    def fl2f(s, pve=ve.n):
        s.wait_ge(ve.h, pve)
        s.activation(statsB[0:1, :], statsB[0:1, :], AF.Ln)
        s.activation(statsB[0:1, :], statsB[0:1, :], AF.Exp,
                     scale=-0.5).then_inc(scs.h, 1)
    S(fl2f)
    n_l2f = scs.inc(1)

    def fl2rc(v, n_l2f=n_l2f):
        v.wait_ge(scs.h, n_l2f)
        vi = v.tensor_tensor(gct[0:1, :], gct[0:1, :], statsB[0:1, :],
                             ALU.mult)
        vi.then_inc(ve.h, 1)
    V(fl2rc)
    n_l2rc = ve.inc(1)

    for ft in range(NFT):
        def fl2bc(t, ft=ft, n_l2rc=n_l2rc, pve=ve.n):
            t.wait_ge(ve.h, max(n_l2rc, pve))
            sl = slice(ft * FT, (ft + 1) * FT)
            t.matmul(pbc[0][:, :], onesrow[:, :], gct[0:1, sl],
                     start=True, stop=True)
            mi = t.matmul(pbc[1][:, :], onesrow[:, :], statsB[0:1, sl],
                          start=True, stop=True)
            mi.then_inc(mm.h, 1)
        T(fl2bc)
        n_lb = mm.inc(1)

        def fl2xl(y, ft=ft, pve=ve.n, pgq=gq.n):
            y.wait_ge(ve.h, pve)
            y.wait_ge(gq.h, pgq)
            y.dma_start(ap3(xt, 0, [[NDB * FT, 128], [FT, NDB], [1, FT]]),
                        x1v[:, :, ft * FT:(ft + 1) * FT]).then_inc(sq.h, 16)
        Y(fl2xl)
        n_lx = sq.inc(16)

        def fl2u(v, ft=ft, n_lb=n_lb, n_lx=n_lx):
            v.wait_ge(mm.h, n_lb)
            v.wait_ge(sq.h, n_lx)
            for k in range(NDB):
                v.tensor_tensor(stg32[0][:, :], xt[:, k * FT:(k + 1) * FT],
                                pbc[1][:, :], ALU.mult)
                with nc.allow_low_precision(reason="ln2 bf16"):
                    vi = v.tensor_tensor(u_ap(k, ft * FT, (ft + 1) * FT),
                                         stg32[0][:, :], pbc[0][:, :],
                                         ALU.subtract)
            vi.then_inc(ve.h, 1)
        V(fl2u)
        ve.inc(1)

    def fl2h(v, pve=ve.n):
        v.wait_ge(ve.h, pve)
        with nc.allow_low_precision(reason="ln2h bf16"):
            vi = v.tensor_copy(S3["aj"][:, 0:128],
                               ap3(ut, TQ, [[NDB * UW, 128], [0, 16],
                                            [UW, 8]]))
        vi.then_inc(ve.h, 1)
    V(fl2h)
    n_hx = ve.inc(1)

    n_gq_h = gq.n

    def fl2hd(g, n_hx=n_hx, n_gq_h=n_gq_h):
        g.wait_ge(ve.h, n_hx)
        g.dma_start(ln2h_loc[:, :], S3["aj"][:, 0:128]).then_inc(gq.h, 16)
        g.wait_ge(gq.h, n_gq_h + 16)
        g.collective_compute(
            "AllGather", ALU.bypass, replica_groups=[list(range(8))],
            ins=[ln2h_loc.ap().opt()], outs=[ln2h_all.ap().opt()],
        ).then_inc(cc.h, 1)
    G(fl2hd)
    gq.inc(16)
    n_cc2 = cc.inc(1)

    def fl2hl(y, n_cc2=n_cc2, pve=ve.n):
        y.wait_ge(cc.h, n_cc2)
        y.wait_ge(ve.h, pve)
        for c in range(8):
            dst = (cp_a if c < 4 else cp_b)
            y.dma_start(dst[:, (c % 4) * 128:(c % 4) * 128 + 128],
                        ln2h_all[c * 128:(c + 1) * 128, :]).then_inc(sq.h, 16)
    Y(fl2hl)
    n_hload = sq.inc(8 * 16)

    def fl2sel(v, n_hload=n_hload):
        v.wait_ge(sq.h, n_hload)
        v.memset(tcf[:, 0:128], 0.0)
        for c in range(8):
            src = (cp_a if c < 4 else cp_b)
            v.scalar_tensor_tensor(tcf[:, 0:128],
                                   src[:, (c % 4) * 128:(c % 4) * 128 + 128],
                                   chsel[:, c:c + 1], tcf[:, 0:128],
                                   ALU.mult, ALU.add)
        with nc.allow_low_precision(reason="halo bf16"):
            vi = v.tensor_copy(ap3(ut, 0, [[NDB * UW, 128], [UW, NDB]]),
                               tcf[:, 0:8])
        vi.then_inc(ve.h, 1)
    V(fl2sel)
    n_usel = ve.inc(1)

    rsig_marks = {}
    kact_marks = {}

    def ev_fr(ft, pb, n_m):
        def fe(s, ft=ft, pb=pb, n_m=n_m, psq=sq.n):
            s.wait_ge(mm.h, n_m)
            s.wait_ge(sq.h, psq)
            s.activation(stg[pb % 2][:, :], pj[pb % 4][:, :],
                         AF.Sigmoid).then_inc(scs.h, 1)
        S(fe)
        n_e = scs.inc(1)

        def fsp(y, ft=ft, pb=pb, n_e=n_e):
            y.wait_ge(scs.h, n_e)
            y.dma_start(s_rsig[:, pb * TQ + ft * FT:pb * TQ + (ft + 1) * FT],
                        stg[pb % 2][:, :]).then_inc(sq.h, 16)
        Y(fsp)
        sq.inc(16)
        rsig_marks[ft] = sq.n
    std_proj_mms(8, "omu_r", ev_fr, uw=n_usel)
    wsel_cycle[WN[10]] = load_weight(WN[10], 0, wbuf_last_mm[0])
    n_rsig_done = sq.n

    def ev_fk(ft, pb, n_m):
        def fe(s, ft=ft, pb=pb, n_m=n_m, psq=sq.n):
            s.wait_ge(mm.h, n_m)
            s.wait_ge(sq.h, psq)
            s.activation(stg32[0][:, :], pj[pb % 4][:, :], AF.Relu)
            s.activation(stg[pb % 2][:, :], stg32[0][:, :],
                         AF.Square).then_inc(scs.h, 1)
        S(fe)
        n_e = scs.inc(1)

        def fsp(y, ft=ft, pb=pb, n_e=n_e):
            y.wait_ge(scs.h, n_e)
            y.dma_start(s_kact[:, pb * TQ + ft * FT:pb * TQ + (ft + 1) * FT],
                        stg[pb % 2][:, :]).then_inc(sq.h, 16)
        Y(fsp)
        sq.inc(16)
        kact_marks[ft] = sq.n
    std_proj_mms(9, "omu_k", ev_fk, uw=n_usel)
    n_kact_done = sq.n

    n_w10 = wsel_cycle[WN[10]]
    for ft in range(NFT):
        def fkl(y, ft=ft, n_kd=kact_marks[ft], bmm=bl_last_mm[ft % 2],
                bve=bl_last_ve[ft % 2]):
            y.wait_ge(sq.h, n_kd)
            y.wait_ge(mm.h, bmm)
            y.wait_ge(ve.h, bve)
            for pb in range(8):
                y.dma_start(bl[ft % 2][:, pb * FT:(pb + 1) * FT],
                            s_kact[:, pb * TQ + ft * FT:
                                   pb * TQ + (ft + 1) * FT]).then_inc(sq.h, 16)
        Y(fkl)
        n_kl = sq.inc(8 * 16)

        def fxl2(y, ft=ft, pve=ve.n, poq=oq.n, n_rd=rsig_marks[ft]):
            y.wait_ge(ve.h, pve)
            y.wait_ge(oq.h, poq)
            y.wait_ge(sq.h, n_rd)
            y.dma_start(ap3(xt, 0, [[NDB * FT, 128], [FT, NDB], [1, FT]]),
                        x1v[:, :, ft * FT:(ft + 1) * FT]).then_inc(sq.h, 16)
        Y(fxl2)
        n_x2 = sq.inc(16)

        for pb in range(8):
            def frl(y, ft=ft, pb=pb, pve=ve.n):
                y.wait_ge(ve.h, pve)
                y.dma_start(stg[pb % 2][:, :],
                            s_rsig[:, pb * TQ + ft * FT:
                                   pb * TQ + (ft + 1) * FT]).then_inc(sq.h, 16)
            Y(frl)
            n_rl = sq.inc(16)

            def fvm(t, ft=ft, pb=pb, n_kl=n_kl, n_w10=n_w10,
                    pve=pjm_ve[pb % 4], pscs=scs.n):
                t.wait_ge(sq.h, n_kl)
                t.wait_ge(gq.h, n_w10)
                t.wait_ge(ve.h, pve)
                t.wait_ge(scs.h, pscs)
                for cb in range(8):
                    mi = t.matmul(pj[pb % 4][:, :],
                                  wbuf[0][:, cb * D + pb * 128:
                                          cb * D + pb * 128 + 128],
                                  bl[ft % 2][:, cb * FT:(cb + 1) * FT],
                                  start=(cb == 0), stop=(cb == 7))
                mi.then_inc(mm.h, 1)
            T(fvm)
            n_vm = mm.inc(1)

            def fout(v, ft=ft, pb=pb, n_vm=n_vm, n_rl=n_rl, n_x2=n_x2):
                v.wait_ge(mm.h, n_vm)
                v.wait_ge(sq.h, max(n_rl, n_x2))
                c0 = pb * FT
                v.tensor_tensor(stg32[pb % 2][:, :], pj[pb % 4][:, :],
                                stg[pb % 2][:, :], ALU.mult)
                vi = v.tensor_tensor(xt[:, c0:c0 + FT], stg32[pb % 2][:, :],
                                     xt[:, c0:c0 + FT], ALU.add)
                vi.then_inc(ve.h, 1)
            V(fout)
            n_ov = ve.inc(1)

            def fosp(y, ft=ft, pb=pb, n_ov=n_ov):
                y.wait_ge(ve.h, n_ov)
                y.dma_start(out_t[pb * 128:(pb + 1) * 128,
                                  ft * FT:(ft + 1) * FT],
                            xt[:, pb * FT:(pb + 1) * FT]).then_inc(oq.h, 16)
            Y(fosp)
            oq.inc(16)
            pjm_ve[pb % 4] = n_ov
        bl_last_mm[ft % 2] = mm.n
        bl_last_ve[ft % 2] = ve.n

    def ffinal(t, n_oq=oq.n):
        t.wait_ge(oq.h, n_oq)
    T(ffinal)

    def realize():
        blk = st.enter_context(nc.Block())

        @blk.sync
        def _(y):
            for f in P["sync"]:
                f(y)

        @blk.gpsimd
        def _(g):
            for f in P["gpsimd"]:
                f(g)

        @blk.tensor
        def _(t):
            for f in P["tensor"]:
                f(t)

        @blk.vector
        def _(v):
            for f in P["vector"]:
                f(v)

        @blk.scalar
        def _(s):
            for f in P["scalar"]:
                f(s)
        st.close()

    ctx = dict(
        nc=nc, P=P, st=st, realize=realize, sems=dict(sq=sq, gq=gq, mm=mm,
        ve=ve, scs=scs, cc=cc), n_p2_done=n_p2_done,
    )
    ctx.update({k: v for k, v in locals().items()
                if not k.startswith("_") and k not in ctx})
    return nc, ctx


def _host_inputs_for_core(core, xs, W):
    b, q = core // 4, core % 4
    x_fm = np.ascontiguousarray(xs[b, q * TQ:(q + 1) * TQ].T)
    if q == 0:
        halo = np.zeros((D,), np.float32)
    else:
        halo = np.ascontiguousarray(xs[b, q * TQ - 1])

    def v128(vec):
        return np.ascontiguousarray(vec.reshape(NDB, 128).T)

    m_strict = np.triu(np.ones((C, C), np.float32), 1)
    m_incl = np.triu(np.ones((C, C), np.float32), 0)
    i64 = np.eye(64, dtype=np.float32)
    irep = np.concatenate([i64, i64], axis=0)
    gnones = np.zeros((128, NDB * 16), np.float32)
    for j in range(NDB):
        gnones[0:64, j * 16 + 2 * j] = 1.0
        gnones[64:128, j * 16 + 2 * j + 1] = 1.0
    gnbc = np.zeros((16, NDB * 128), np.float32)
    for j in range(NDB):
        gnbc[2 * j, j * 128 + 0:j * 128 + 64] = 1.0
        gnbc[2 * j + 1, j * 128 + 64:j * 128 + 128] = 1.0
    csel = np.zeros((128, 8), np.float32)
    csel[:, core] = 1.0
    hsel = np.zeros((128, 8), np.float32)
    if q > 0:
        hsel[:, core - 1] = 1.0

    im = {
        "x_in": x_fm,
        "xhalo": v128(halo),
        "masks_in": np.tile(np.concatenate(
            [m_strict, m_incl, m_strict.T, m_incl.T], axis=1), (2, 1)),
        "irep_in": irep,
        "gnones_in": gnones,
        "gnbc_in": gnbc,
        "csel_in": csel,
        "hsel_in": hsel,
    }
    for n, key in [("mx_r", "x_r"), ("mx_w", "x_w"), ("mx_k", "x_k"),
                   ("mx_v", "x_v"), ("mx_a", "x_a"), ("mx_b", "x_b"),
                   ("mx_g", "x_g"), ("mu_r", "mu_r"), ("mu_k", "mu_k"),
                   ("gn_w", "gn_w"), ("gn_b", "gn_b"), ("b_o", "b_o")]:
        im["v_" + n] = v128(W[key])
    for wn in ["W_w", "W_v", "W_r", "W_k", "W_a", "W_b", "W_g", "W_o",
               "fW_r", "fW_k", "fW_v"]:
        im["w_" + wn] = np.ascontiguousarray(W[wn].T)
    return im


def kernel(**inputs):
    xs = np.asarray(inputs["L0_attn_x"], np.float32)
    W = {k.replace("L0_attn_", "").replace("L0_ffn_W", "fW").replace(
        "L0_ffn_", ""): np.asarray(v, np.float32)
        for k, v in inputs.items() if k != "L0_attn_x"}
    ims = [_host_inputs_for_core(c, xs, W) for c in range(8)]
    nc, ctx = build_program(dev_dump=False)
    ctx["realize"]()
    res = run_bass_kernel_spmd(nc, ims, core_ids=list(range(8)))
    out = np.zeros((2, 4 * TQ, D), np.float32)
    for c in range(8):
        b, q = c // 4, c % 4
        ot = np.asarray(res.results[c]["out_t"], np.float32)
        out[b, q * TQ:(q + 1) * TQ, :] = ot.T
    return out


# version 07
# speedup vs baseline: 1.1043x; 1.1043x over previous
import contextlib
import numpy as np
import ml_dtypes
import concourse.bass as bass
import concourse.mybir as mybir
from concourse.bass_utils import run_bass_kernel_spmd

BF16 = ml_dtypes.bfloat16
F32 = mybir.dt.float32
BF = mybir.dt.bfloat16
AF = mybir.ActivationFunctionType
ALU = mybir.AluOpType
AX = mybir.AxisListType

D, H, KD, C = 1024, 16, 64, 64
TQ = 2048
NCH = TQ // C
FT = 512
NFT = TQ // FT
NDB = 8
UW = TQ + 8
LN_EPS, GN_EPS, NORM_EPS = 1e-5, 0.00064, 1e-6


class Sem:
    def __init__(self, h):
        self.h = h
        self.n = 0

    def inc(self, k=1):
        self.n += k
        return self.n


def build_program(dev_dump=False):
    nc = bass.Bass("TRN2", target_bir_lowering=False)
    dt = nc.dram_tensor

    x_in = dt("x_in", [D, TQ], F32, kind="ExternalInput")
    xhalo = dt("xhalo", [128, NDB], F32, kind="ExternalInput")
    WN = ["W_w", "W_v", "W_r", "W_k", "W_a", "W_b", "W_g", "W_o",
          "fW_r", "fW_k", "fW_v"]
    wt = {n: dt("w_" + n, [D, D], F32, kind="ExternalInput") for n in WN}
    VN = ["mx_r", "mx_w", "mx_k", "mx_v", "mx_a", "mx_b", "mx_g",
          "mu_r", "mu_k", "gn_w", "gn_b", "b_o"]
    vecs = {n: dt("v_" + n, [128, NDB], F32, kind="ExternalInput") for n in VN}
    masks_in = dt("masks_in", [128, 4 * C], F32, kind="ExternalInput")
    irep_in = dt("irep_in", [128, C], F32, kind="ExternalInput")
    gnones_in = dt("gnones_in", [128, NDB * 16], F32, kind="ExternalInput")
    gnbc_in = dt("gnbc_in", [16, NDB * 128], F32, kind="ExternalInput")
    csel_in = dt("csel_in", [128, 8], F32, kind="ExternalInput")
    hsel_in = dt("hsel_in", [128, 8], F32, kind="ExternalInput")
    out_t = dt("out_t", [D, TQ], F32, kind="ExternalOutput")

    ks = "ExternalOutput" if dev_dump else "Internal"

    def scr(name, shape, dtyp):
        return dt(name, shape, dtyp, kind=ks)

    s_strip = {n: scr("s_" + n, [128, NDB * TQ], BF) for n in
               ("ah", "bh", "kh", "rh", "araw", "braw", "g", "eA", "eBK",
                "eR")}
    s_v = scr("s_v", [128, (TQ // 128) * D], BF)
    s_sp = scr("s_sp", [128, NDB * TQ], F32)
    s_zt = scr("s_zt", [128, NCH * 8 * C], BF)
    s_mbBR = scr("s_mbBR", [128, NCH * 8 * C], BF)
    s_btok = scr("s_btok", [128, NCH * 8 * C], BF)
    s_o = scr("s_o", [128, NDB * TQ], BF)
    s_x1 = scr("s_x1", [D, TQ], F32)
    s_rsig = scr("s_rsig", [128, NDB * TQ], BF)
    s_kact = scr("s_kact", [128, NDB * TQ], BF)
    sum_loc = dt("sum_loc", [128, 2 * 8 * C], BF)
    sum_all = dt("sum_all", [8 * 128, 2 * 8 * C], BF)
    s_o0 = scr("s_o0", [128, NDB * TQ], BF)
    s_dbg = scr("s_dbg", [128, 8 * 8 * C], F32)
    ln2h_loc = dt("ln2h_loc", [128, 128], BF)
    ln2h_all = dt("ln2h_all", [8 * 128, 128], BF)

    P = {e: [] for e in ("tensor", "vector", "scalar", "gpsimd", "sync")}
    st = contextlib.ExitStack()

    def sb(name, shape, dtyp):
        return st.enter_context(nc.sbuf_tensor(name, shape, dtyp))

    def psum(name, shape, dtyp=F32):
        return st.enter_context(nc.psum_tensor(name, shape, dtyp))

    def sem(name):
        return Sem(st.enter_context(nc.semaphore(name)))

    cvec = {n: sb("c_" + n, [128, NDB], F32) for n in VN}
    omu_r = sb("omu_r", [128, NDB], F32)
    omu_k = sb("omu_k", [128, NDB], F32)
    cvec["omu_r"] = omu_r
    cvec["omu_k"] = omu_k
    cmask = sb("cmask", [128, 4 * C], BF)
    cirep = sb("cirep", [128, C], BF)
    cgno = sb("cgno", [128, NDB * 16], BF)
    cgnb = sb("cgnb", [16, NDB * 128], BF)
    ccsel = sb("ccsel", [128, 8], F32)
    chsel = sb("chsel", [128, 8], F32)
    whalo = sb("whalo", [128, NDB], F32)
    onescol = sb("onescol", [128, 1], BF)
    onescol32 = sb("onescol32", [128, 1], F32)
    onesrow = sb("onesrow", [1, 128], F32)
    onesrow_b = sb("onesrow_b", [1, 128], BF)
    hsq = sb("hsq", [128, NDB], F32)

    xt = sb("xt", [128, NDB * FT], F32)
    xbf = sb("xbf", [128, NDB * FT], BF)
    ut = sb("ut", [128, NDB * UW], BF)
    bl = [sb(f"bl{i}", [128, NDB * FT], BF) for i in range(2)]
    wbuf = [sb(f"wbuf{i}", [128, NDB * D], BF) for i in range(2)]
    eAd = [sb("eAd0", [128, TQ], BF)]
    spt = sb("spt", [128, TQ], F32)
    gct = sb("gct", [128, TQ], F32)
    gmask = sb("gmask", [128, TQ], BF)
    statsB = sb("statsB", [16, TQ], F32)
    hstat = sb("hstat", [1, 16], F32)
    rstdab = sb("rstdab", [16, 2 * TQ], BF)
    stg = [sb(f"stg{i}", [128, FT], BF) for i in range(4)]
    stg32 = [sb(f"stg32_{i}", [128, FT], F32) for i in range(2)]
    ehcol = sb("ehcol", [128, NDB * (NCH + 1)], F32)
    eecol = sb("eecol", [128, NDB * NCH], F32)

    CW = 8 * C
    scn = ("nt", "mbbr", "mbkr", "atok", "btk", "ktk",
           "aj", "tj", "aj2", "tj2", "ztt", "kt", "u0")
    S3 = {n: sb("p3_" + n, [128, CW], BF) for n in scn}
    x0b = sb("x0b", [128, 2 * CW], BF)
    x1b = sb("x1b", [128, 2 * CW], BF)
    tcb = sb("tcb", [128, CW], BF)
    tcf = sb("tcf", [128, CW], F32)
    xqb = sb("xqb", [128, CW], BF)
    xqf = sb("xqf", [128, CW], F32)
    stw = {n: [sb(f"stw_{n}{i}", [128, NDB * C], BF) for i in range(2)]
           for n in ("a", "b", "k", "r")}
    vw = [sb(f"vw{i}", [128, D], BF) for i in range(2)]
    cwin = {n: [sb(f"cw_{n}{i}", [128, CW], BF) for i in range(2)]
            for n in ("zt", "br", "bt")}
    cp_a = sb("cp_a", [128, 8 * C], BF)
    cp_b = sb("cp_b", [128, 8 * C], BF)
    cp_sin = sb("cp_sin", [128, 8 * C], F32)
    oc = [sb(f"oc{i}", [128, 8 * C], BF) for i in range(2)]

    pj = [psum(f"pj{i}", [128, FT]) for i in range(4)]
    pst = psum("pst", [128, FT])
    pbc = [psum(f"pbc{i}", [128, FT]) for i in range(2)]
    pex = psum("pex", [128, FT])

    sq = sem("sq")
    gq = sem("gq")
    mm = sem("mm")
    ve = sem("ve")
    scs = sem("scs")
    cc = sem("cc")
    oq = sem("oq")

    xv = x_in.rearrange("(b p) t -> p b t", p=128)
    outv = out_t.rearrange("(b p) t -> p b t", p=128)
    x1v = s_x1.rearrange("(b p) t -> p b t", p=128)
    wvw = {n: wt[n].rearrange("(b p) e -> p b e", p=128) for n in WN}

    def T(f):
        P["tensor"].append(f)

    def V(f):
        P["vector"].append(f)

    def S(f):
        P["scalar"].append(f)

    def G(f):
        P["gpsimd"].append(f)

    def Y(f):
        P["sync"].append(f)

    def ap3(t, off, dims):
        return bass.AP(t, off, dims)

    def u_ap(k, a, b):
        return ut[:, k * UW + 1 + a: k * UW + 1 + b]

    def uprev_ap(k, a, b):
        return ut[:, k * UW + a: k * UW + b]

    def fy0(y):
        for n in VN:
            y.dma_start(cvec[n][:, :], vecs[n][:, :]).then_inc(sq.h, 16)
        y.dma_start(ccsel[:, :], csel_in[:, :]).then_inc(sq.h, 16)
        y.dma_start(chsel[:, :], hsel_in[:, :]).then_inc(sq.h, 16)
        y.dma_start(whalo[:, :], xhalo[:, :]).then_inc(sq.h, 16)
    Y(fy0)
    sq.inc(16 * (len(VN) + 3))
    n_setup_sq = sq.n

    def fg0(g):
        g.dma_start(cmask[:, :], masks_in[:, :]).then_inc(gq.h, 16)
        g.dma_start(cirep[:, :], irep_in[:, :]).then_inc(gq.h, 16)
        g.dma_start(cgno[:, :], gnones_in[:, :]).then_inc(gq.h, 16)
        g.dma_start(cgnb[:, :], gnbc_in[:, :]).then_inc(gq.h, 16)
    G(fg0)
    gq.inc(64)
    n_setup_gq = gq.n

    def fv0(v, n=n_setup_sq):
        v.memset(onescol[:, :], 1.0)
        v.memset(onescol32[:, :], 1.0)
        v.memset(onesrow[:, :], 1.0)
        v.memset(onesrow_b[:, :], 1.0)
        v.memset(gmask[:, :], 1.0)
        v.memset(hstat[0:1, 7:8], LN_EPS)
        v.memset(ap3(gmask, 0, [[TQ, 128], [C, NCH]]), 0.0)
        v.wait_ge(sq.h, n)
        v.tensor_scalar(omu_r[:, :], cvec["mu_r"][:, :], -1.0, 1.0,
                        ALU.mult, ALU.add)
        v.tensor_scalar(omu_k[:, :], cvec["mu_k"][:, :], -1.0, 1.0,
                        ALU.mult, ALU.add).then_inc(ve.h, 1)
    V(fv0)
    n_setup_ve = ve.inc(1)

    def load_weight(name, bi, wait_mm=0):
        def f(g, name=name, bi=bi, wait_mm=wait_mm):
            g.wait_ge(mm.h, wait_mm)
            g.dma_start(ap3(wbuf[bi], 0, [[NDB * D, 128], [D, NDB], [1, D]]),
                        wvw[name]).then_inc(gq.h, 16)
        G(f)
        return gq.inc(16)

    n_w0 = load_weight(WN[0], 0)

    n_x32 = []
    n_var = []
    last_u_ve = [n_setup_ve]
    for ft in range(NFT):
        def fy(y, ft=ft):
            y.dma_start(ap3(xt, 0, [[NDB * FT, 128], [FT, NDB], [1, FT]]),
                        xv[:, :, ft * FT:(ft + 1) * FT]).then_inc(sq.h, 16)
        def fgl(g, ft=ft, pmm=mm.n):
            g.wait_ge(mm.h, pmm)
            g.dma_start(ap3(xbf, 0, [[NDB * FT, 128], [FT, NDB], [1, FT]]),
                        xv[:, :, ft * FT:(ft + 1) * FT]).then_inc(gq.h, 16)
        G(fgl)
        n_xb = gq.inc(16)

        prev_sc = scs.n

        def fm(t, n_xb=n_xb, prev=ve.n, prev_sc=prev_sc):
            t.wait_ge(gq.h, n_xb)
            t.wait_ge(ve.h, prev)
            t.wait_ge(scs.h, prev_sc)
            for k in range(NDB):
                mi = t.matmul(pst[0:1, :], onescol[:, :],
                              xbf[:, k * FT:(k + 1) * FT],
                              start=(k == 0), stop=(k == NDB - 1))
            mi.then_inc(mm.h, 1)
        T(fm)
        n_mean = mm.inc(1)

        def fsq(s, n_mean=n_mean):
            s.wait_ge(mm.h, n_mean)
            s.activation(xbf[:, :], xbf[:, :], AF.Square).then_inc(scs.h, 1)
        S(fsq)
        n_sq = scs.inc(1)

        def fms(t, n_sq=n_sq):
            t.wait_ge(scs.h, n_sq)
            for k in range(NDB):
                mi = t.matmul(pst[32:33, :], onescol[:, :],
                              xbf[:, k * FT:(k + 1) * FT],
                              start=(k == 0), stop=(k == NDB - 1))
            mi.then_inc(mm.h, 1)
        T(fms)
        n_msq = mm.inc(1)

        def fst(v, ft=ft, n_msq=n_msq):
            v.wait_ge(mm.h, n_msq)
            sl = slice(ft * FT, (ft + 1) * FT)
            v.tensor_scalar_mul(gct[0:1, sl], pst[0:1, :], 1.0 / D)
            v.tensor_scalar_mul(spt[0:1, 0:FT], pst[32:33, :], 1.0 / D)
            v.tensor_tensor(spt[0:1, FT:2 * FT], gct[0:1, sl],
                            gct[0:1, sl], ALU.mult)
            v.tensor_tensor(statsB[0:1, sl], spt[0:1, 0:FT],
                            spt[0:1, FT:2 * FT], ALU.subtract)
            v.tensor_scalar_add(statsB[0:1, sl], statsB[0:1, sl],
                                LN_EPS).then_inc(ve.h, 1)
        V(fst)
        n_var.append(ve.inc(1))

    def fhm(t, n=n_setup_sq, nv=ve.n):
        t.wait_ge(sq.h, n)
        t.wait_ge(ve.h, nv)
        mi = t.matmul(pst[0:1, 0:NDB], onescol32[:, :], whalo[:, :],
                      start=True, stop=True)
        mi.then_inc(mm.h, 1)
    T(fhm)
    n_hm = mm.inc(1)

    def fhsq(v, n_hm=n_hm):
        v.wait_ge(mm.h, n_hm)
        v.tensor_tensor(hsq[:, :], whalo[:, :], whalo[:, :],
                        ALU.mult).then_inc(ve.h, 1)
    V(fhsq)
    n_hsq = ve.inc(1)

    def fhm2(t, n_hsq=n_hsq):
        t.wait_ge(ve.h, n_hsq)
        mi = t.matmul(pst[32:33, 0:NDB], onescol32[:, :], hsq[:, :],
                      start=True, stop=True)
        mi.then_inc(mm.h, 1)
    T(fhm2)
    n_hm2 = mm.inc(1)

    def fhv(v, n_hm2=n_hm2):
        v.wait_ge(mm.h, n_hm2)
        v.reduce_sum(hstat[0:1, 0:1], pst[0:1, 0:NDB], AX.X)
        vi = v.reduce_sum(hstat[0:1, 3:4], pst[32:33, 0:NDB], AX.X)
        vi.then_inc(ve.h, 1)
    V(fhv)
    n_hv0 = ve.inc(1)

    def fhs2(s, n_hv0=n_hv0):
        s.wait_ge(ve.h, n_hv0)
        s.activation(hstat[0:1, 1:2], hstat[0:1, 0:1], AF.Copy,
                     scale=1.0 / D)
        s.activation(hstat[0:1, 2:3], hstat[0:1, 1:2], AF.Square)
        si = s.activation(hstat[0:1, 3:4], hstat[0:1, 3:4], AF.Copy,
                          scale=1.0 / D)
        si.then_inc(scs.h, 1)
    S(fhs2)
    n_hs2 = scs.inc(1)

    def fhv2(v, n_hs2=n_hs2):
        v.wait_ge(scs.h, n_hs2)
        vi = v.tensor_tensor(hstat[0:1, 3:4], hstat[0:1, 3:4],
                             hstat[0:1, 2:3], ALU.subtract)
        vi.then_inc(ve.h, 1)
    V(fhv2)
    n_hv = ve.inc(1)

    def fsrt(s, n_hv=n_hv):
        s.wait_ge(ve.h, n_hv)
        s.activation(statsB[0:1, :], statsB[0:1, :], AF.Ln)
        s.activation(statsB[0:1, :], statsB[0:1, :], AF.Exp, scale=-0.5)
        s.activation(hstat[0:1, 3:4], hstat[0:1, 3:4], AF.Ln,
                     bias=hstat[0:1, 7:8])
        s.activation(hstat[0:1, 4:5], hstat[0:1, 3:4], AF.Exp,
                     scale=-0.5).then_inc(scs.h, 1)
    S(fsrt)
    n_srt = scs.inc(1)

    def frc(v, n_srt=n_srt):
        v.wait_ge(scs.h, n_srt)
        v.tensor_tensor(gct[0:1, :], gct[0:1, :], statsB[0:1, :], ALU.mult)
        v.tensor_tensor(hstat[0:1, 5:6], hstat[0:1, 1:2],
                        hstat[0:1, 4:5], ALU.mult).then_inc(ve.h, 1)
    V(frc)
    n_rc = ve.inc(1)

    for ft in range(NFT):
        def fbc(t, ft=ft, n_rc=n_rc, pve=ve.n):
            t.wait_ge(ve.h, max(n_rc, pve))
            sl = slice(ft * FT, (ft + 1) * FT)
            t.matmul(pbc[0][:, :], onesrow[:, :], gct[0:1, sl],
                     start=True, stop=True)
            mi = t.matmul(pbc[1][:, :], onesrow[:, :], statsB[0:1, sl],
                          start=True, stop=True)
            mi.then_inc(mm.h, 1)
        T(fbc)
        n_bc = mm.inc(1)

        def fy(y, ft=ft, pve=ve.n):
            y.wait_ge(ve.h, pve)
            y.dma_start(ap3(xt, 0, [[NDB * FT, 128], [FT, NDB], [1, FT]]),
                        xv[:, :, ft * FT:(ft + 1) * FT]).then_inc(sq.h, 16)
        Y(fy)
        n_x = sq.inc(16)

        def fu(v, ft=ft, n_bc=n_bc, n_x=n_x):
            v.wait_ge(mm.h, n_bc)
            v.wait_ge(sq.h, n_x)
            for k in range(NDB):
                v.tensor_tensor(stg32[0][:, :], xt[:, k * FT:(k + 1) * FT],
                                pbc[1][:, :], ALU.mult)
                vi = v.tensor_tensor(u_ap(k, ft * FT, (ft + 1) * FT),
                                     stg32[0][:, :], pbc[0][:, :], ALU.subtract)
            vi.then_inc(ve.h, 1)
        V(fu)
        ve.inc(1)

    def fhb(t, n_rc=n_rc, pve=ve.n):
        t.wait_ge(ve.h, max(n_rc, pve))
        t.matmul(pbc[0][:, 0:1], onesrow[:, :], hstat[0:1, 5:6],
                 start=True, stop=True)
        mi = t.matmul(pbc[1][:, 0:1], onesrow[:, :], hstat[0:1, 4:5],
                      start=True, stop=True)
        mi.then_inc(mm.h, 1)
    T(fhb)
    n_hb = mm.inc(1)

    def fhu(v, n_hb=n_hb):
        v.wait_ge(mm.h, n_hb)
        v.tensor_scalar(hsq[:, :], whalo[:, :], pbc[1][:, 0:1],
                        pbc[0][:, 0:1], ALU.mult, ALU.subtract)
        vi = v.tensor_scalar(ap3(ut, 0, [[NDB * UW, 128], [UW, NDB]]),
                             whalo[:, :], pbc[1][:, 0:1],
                             pbc[0][:, 0:1], ALU.mult, ALU.subtract)
        vi.then_inc(ve.h, 1)
    V(fhu)
    n_uall = ve.inc(1)

    if dev_dump:
        def fdbgh(g, n_uall=n_uall):
            g.wait_ge(ve.h, n_uall)
            g.dma_start(s_dbg[:, 6 * CW:6 * CW + NDB], hsq[:, :]).then_inc(gq.h, 16)
            g.dma_start(s_dbg[:, 6 * CW + 16:6 * CW + 16 + NDB],
                        ap3(ut, 0, [[NDB * UW, 128], [UW, NDB], [1, 1]]),
                        allow_slow_non_contiguous=True).then_inc(gq.h, 16)
            g.dma_start(s_dbg[0:1, 6 * CW + 32:6 * CW + 48],
                        hstat[0:1, 0:16]).then_inc(gq.h, 16)
            g.dma_start(s_dbg[:, 6 * CW + 64:6 * CW + 64 + NDB],
                        whalo[:, :]).then_inc(gq.h, 16)
        G(fdbgh)
        gq.inc(64)


    def blend_tiles(proj_i, mixname, ft, bli, waits_ve, wait_mm=0):
        def fb(v, mixname=mixname, ft=ft, bli=bli, waits_ve=waits_ve,
               wait_mm=wait_mm):
            v.wait_ge(ve.h, waits_ve)
            v.wait_ge(mm.h, wait_mm)
            for k in range(NDB):
                v.tensor_tensor(stg32[1][:, :],
                                uprev_ap(k, ft * FT, (ft + 1) * FT),
                                u_ap(k, ft * FT, (ft + 1) * FT), ALU.subtract)
                vi = v.scalar_tensor_tensor(
                    bl[bli][:, k * FT:(k + 1) * FT], stg32[1][:, :],
                    cvec[mixname][:, k:k + 1],
                    u_ap(k, ft * FT, (ft + 1) * FT), ALU.mult, ALU.add)
            vi.then_inc(ve.h, 1)
        V(fb)
        return ve.inc(1)

    def spill(eng_append, dram, col_off, src, wsem, wn):
        def f(y, dram=dram, col_off=col_off, src=src, wsem=wsem, wn=wn):
            y.wait_ge(wsem.h, wn)
            y.dma_start(dram[:, col_off:col_off + FT], src[:, :]).then_inc(
                sq.h, 16)
        eng_append(f)
        return sq.inc(16)

    n_wnext = load_weight(WN[1], 1)
    bl_last_mm = [0, 0]
    bl_last_ve = [0, 0]
    wbuf_last_mm = [0, 0]
    pjm_ve = [0, 0, 0, 0]
    pjm_scs = [0, 0, 0, 0]

    def std_proj_mms(widx, mixname, evac_after, uw=None):
        name = WN[widx]
        n_w = wsel_cycle[name]
        for ft in range(NFT):
            nb = blend_tiles(widx, mixname, ft, ft % 2,
                             n_uall if uw is None else uw,
                             bl_last_mm[ft % 2])
            for pb in range(8):
                def fmm(t, ft=ft, pb=pb, nb=nb, n_w=n_w,
                        pve=pjm_ve[pb % 4], psc=pjm_scs[pb % 4],
                        widx=widx):
                    t.wait_ge(ve.h, nb)
                    t.wait_ge(gq.h, n_w)
                    t.wait_ge(ve.h, pve)
                    t.wait_ge(scs.h, psc)
                    for cb in range(8):
                        mi = t.matmul(pj[pb % 4][:, :],
                                      wbuf[widx % 2][:, cb * D + pb * 128:
                                                     cb * D + pb * 128 + 128],
                                      bl[ft % 2][:, cb * FT:(cb + 1) * FT],
                                      start=(cb == 0), stop=(cb == 7))
                    mi.then_inc(mm.h, 1)
                T(fmm)
                n_m = mm.inc(1)
                evac_after(ft, pb, n_m)
                pjm_ve[pb % 4] = ve.n
                pjm_scs[pb % 4] = scs.n
            bl_last_mm[ft % 2] = mm.n
            bl_last_ve[ft % 2] = ve.n
            wbuf_last_mm[widx % 2] = mm.n

    wsel_cycle = {WN[0]: n_w0, WN[1]: n_wnext}

    def ev_ww(ft, pb, n_m):
        def fe(s, ft=ft, pb=pb, n_m=n_m, psq=sq.n):
            s.wait_ge(mm.h, n_m)
            s.wait_ge(sq.h, psq)
            s.activation(stg32[0][:, :], pj[pb % 4][:, :], AF.Exp)
            s.activation(stg32[0][:, :], stg32[0][:, :], AF.Ln,
                         bias=1.0).then_inc(scs.h, 1)
        S(fe)
        n_e = scs.inc(1)

        def fsp(y, ft=ft, pb=pb, n_e=n_e):
            y.wait_ge(scs.h, n_e)
            y.dma_start(s_sp[:, pb * TQ + ft * FT:pb * TQ + (ft + 1) * FT],
                        stg32[0][:, :]).then_inc(sq.h, 16)
        Y(fsp)
        sq.inc(16)

    std_proj_mms(0, "mx_w", ev_ww)

    gam_done = []
    for pb in range(8):
        ebuf = 0

        def fld(y, pb=pb, psc=scs.n):
            y.wait_ge(scs.h, psc)
            y.dma_start(spt[:, :], s_sp[:, pb * TQ:(pb + 1) * TQ]
                        ).then_inc(sq.h, 16)
        Y(fld)
        n_l = sq.inc(16)

        def fsc(v, pb=pb, n_l=n_l, psc=scs.n):
            v.wait_ge(sq.h, n_l)
            v.wait_ge(scs.h, psc)
            v.tensor_tensor_scan(gct[:, :], gmask[:, :], spt[:, :], 0.0,
                                 ALU.mult, ALU.add)
            vi = v.tensor_scalar_mul(
                ehcol[:, pb * (NCH + 1):pb * (NCH + 1) + NCH],
                ap3(gct, C - 1, [[TQ, 128], [C, NCH]]), 0.5)
            vi.then_inc(ve.h, 1)
        V(fsc)
        n_gc = ve.inc(1)

        def fsh(v, pb=pb, n_gc=n_gc):
            v.wait_ge(ve.h, n_gc)
            hb = ap3(ehcol, pb * (NCH + 1), [[NDB * (NCH + 1), 128],
                                             [1, NCH], [0, C]])
            g3 = ap3(gct, 0, [[TQ, 128], [C, NCH], [1, C]])
            s3 = ap3(spt, 0, [[TQ, 128], [C, NCH], [1, C]])
            v.tensor_tensor(g3, g3, hb, ALU.subtract)
            vi = v.tensor_tensor(spt[:, :], spt[:, :], gct[:, :], ALU.subtract)
            vi.then_inc(ve.h, 1)
        V(fsh)
        n_sh = ve.inc(1)

        def fex1(s, pb=pb, n_sh=n_sh, pgq=gq.n, pve=ve.n):
            s.wait_ge(ve.h, n_sh)
            s.wait_ge(ve.h, pve)
            s.wait_ge(gq.h, pgq)
            s.activation(eAd[0][:, :], gct[:, :], AF.Exp).then_inc(scs.h, 1)
        S(fex1)
        n_ex1 = scs.inc(1)

        def fspl1(g, pb=pb, n_ex1=n_ex1):
            g.wait_ge(scs.h, n_ex1)
            g.dma_start(s_strip["eBK"][:, pb * TQ:(pb + 1) * TQ],
                        eAd[0][:, :]).then_inc(gq.h, 16)
        G(fspl1)
        n_sp1 = gq.inc(16)

        def fex(s, pb=pb, n_sp1=n_sp1):
            s.wait_ge(gq.h, n_sp1)
            s.activation(eAd[0][:, :], spt[:, :], AF.Exp).then_inc(scs.h, 1)
        S(fex)
        n_ex = scs.inc(1)

        def fexr(s, pb=pb, n_ex=n_ex):
            s.activation(spt[:, :], gct[:, :], AF.Exp,
                         scale=-1.0).then_inc(scs.h, 1)
        S(fexr)
        n_exr = scs.inc(1)

        def fsplr(g, pb=pb, n_exr=n_exr):
            g.wait_ge(scs.h, n_exr)
            g.dma_start(ap3(s_strip["eR"], pb * TQ,
                            [[NDB * TQ, 128], [1, TQ]]).bitcast(BF)
                        if False else s_strip["eR"][:, pb * TQ:(pb + 1) * TQ],
                        spt[:, :]).then_inc(gq.h, 16)
        G(fsplr)
        gq.inc(16)

        def fspl(g, pb=pb, n_ex=n_ex):
            g.wait_ge(scs.h, n_ex)
            g.dma_start(s_strip["eA"][:, pb * TQ:(pb + 1) * TQ],
                        eAd[0][:, :]).then_inc(gq.h, 16)
        G(fspl)
        gq.inc(16)

        def feh(v, pb=pb, n_exr=n_exr):
            v.wait_ge(scs.h, n_exr)
            base = pb * (NCH + 1)
            v.memset(ehcol[:, base + NCH:base + NCH + 1], 0.0)
            vi = v.tensor_tensor(eecol[:, pb * NCH:(pb + 1) * NCH],
                                 ehcol[:, base:base + NCH],
                                 ehcol[:, base + 1:base + NCH + 1], ALU.add)
            vi.then_inc(ve.h, 1)
        V(feh)
        n_feh = ve.inc(1)

        def fehx(s, pb=pb, n_feh=n_feh):
            s.wait_ge(ve.h, n_feh)
            base = pb * (NCH + 1)
            s.activation(ehcol[:, base:base + NCH],
                         ehcol[:, base:base + NCH], AF.Exp, scale=-1.0)
            s.activation(eecol[:, pb * NCH:(pb + 1) * NCH],
                         eecol[:, pb * NCH:(pb + 1) * NCH], AF.Exp,
                         scale=-1.0).then_inc(scs.h, 1)
        S(fehx)
        scs.inc(1)
        gam_done.append(ve.n)
    n_gamma = gam_done[-1]
    n_gamma_sc = scs.n

    def emit_wv():
        name = "W_v"
        n_w = wsel_cycle[name]
        wsel_cycle[WN[2]] = load_weight(WN[2], 0, wbuf_last_mm[0])
        for ft in range(NFT):
            nb = blend_tiles(1, "mx_v", ft, ft % 2, n_uall,
                             bl_last_mm[ft % 2])
            for tb in range(4):
                for eh in range(2):
                    def fmm(t, ft=ft, tb=tb, eh=eh, nb=nb, n_w=n_w,
                            pve=pjm_ve[(tb * 2 + eh) % 4]):
                        t.wait_ge(ve.h, nb)
                        t.wait_ge(gq.h, n_w)
                        t.wait_ge(ve.h, pve)
                        for cb in range(8):
                            mi = t.matmul(
                                pj[(tb * 2 + eh) % 4][:, :],
                                bl[ft % 2][:, cb * FT + tb * 128:
                                           cb * FT + (tb + 1) * 128],
                                wbuf[1][:, cb * D + eh * FT:
                                        cb * D + (eh + 1) * FT],
                                start=(cb == 0), stop=(cb == 7))
                        mi.then_inc(mm.h, 1)
                    T(fmm)
                    n_m = mm.inc(1)

                    def fev(v, ft=ft, tb=tb, eh=eh, n_m=n_m, psq=sq.n):
                        v.wait_ge(mm.h, n_m)
                        v.wait_ge(sq.h, psq)
                        vi = v.tensor_copy(stg[(tb * 2 + eh) % 4][:, :],
                                           pj[(tb * 2 + eh) % 4][:, :])
                        vi.then_inc(ve.h, 1)
                    V(fev)
                    n_e = ve.inc(1)
                    def fsp(y, ft=ft, tb=tb, eh=eh, n_e=n_e):
                        y.wait_ge(ve.h, n_e)
                        gtb = ft * 4 + tb
                        y.dma_start(s_v[:, gtb * D + eh * FT:
                                        gtb * D + (eh + 1) * FT],
                                    stg[(tb * 2 + eh) % 4][:, :]).then_inc(
                            sq.h, 16)
                    Y(fsp)
                    sq.inc(16)
                    pjm_ve[(tb * 2 + eh) % 4] = ve.n
            bl_last_mm[ft % 2] = mm.n
            bl_last_ve[ft % 2] = ve.n
            wbuf_last_mm[1] = mm.n
    emit_wv()

    def emit_wr():
        name = "W_r"
        n_w = wsel_cycle[name]
        wsel_cycle[WN[3]] = load_weight(WN[3], 1, wbuf_last_mm[1])
        for ft in range(NFT):
            nb = blend_tiles(2, "mx_r", ft, ft % 2, n_uall,
                             bl_last_mm[ft % 2])
            for pb in range(8):
                def fld(y, ft=ft, pb=pb, pve=ve.n, pgq=gq.n):
                    y.wait_ge(ve.h, pve)
                    y.wait_ge(gq.h, pgq)
                    y.dma_start(stg[3][:, :],
                                s_strip["eR"][:, pb * TQ + ft * FT:
                                              pb * TQ + (ft + 1) * FT]
                                ).then_inc(sq.h, 16)
                Y(fld)
                n_l = sq.inc(16)

                def fmm(t, ft=ft, pb=pb, nb=nb, n_w=n_w,
                        pve=pjm_ve[pb % 4]):
                    t.wait_ge(ve.h, nb)
                    t.wait_ge(gq.h, n_w)
                    t.wait_ge(ve.h, pve)
                    for cb in range(8):
                        mi = t.matmul(pj[pb % 4][:, :],
                                      wbuf[0][:, cb * D + pb * 128:
                                              cb * D + pb * 128 + 128],
                                      bl[ft % 2][:, cb * FT:(cb + 1) * FT],
                                      start=(cb == 0), stop=(cb == 7))
                    mi.then_inc(mm.h, 1)
                T(fmm)
                n_m = mm.inc(1)

                def fev(v, ft=ft, pb=pb, n_m=n_m, n_l=n_l, psq=sq.n):
                    v.wait_ge(mm.h, n_m)
                    v.wait_ge(sq.h, max(n_l, psq))
                    v.tensor_tensor(stg[pb % 2][:, :], pj[pb % 4][:, :],
                                    stg[3][:, :], ALU.mult).then_inc(ve.h, 1)
                V(fev)
                n_e = ve.inc(1)
                spill(Y, s_strip["rh"], pb * TQ + ft * FT, stg[pb % 2],
                      ve, n_e)
                pjm_ve[pb % 4] = ve.n
            bl_last_mm[ft % 2] = mm.n
            bl_last_ve[ft % 2] = ve.n
            wbuf_last_mm[0] = mm.n
    emit_wr()

    def emit_wk():
        name = "W_k"
        n_w = wsel_cycle[name]
        wsel_cycle[WN[4]] = load_weight(WN[4], 0, wbuf_last_mm[0])
        for ft in range(NFT):
            nb = blend_tiles(3, "mx_k", ft, ft % 2, n_uall,
                             bl_last_mm[ft % 2])
            for pb in range(8):
                def fld(y, ft=ft, pb=pb, pve=ve.n, pgq=gq.n):
                    y.wait_ge(ve.h, pve)
                    y.wait_ge(gq.h, pgq)
                    y.dma_start(stg[3][:, :],
                                s_strip["eBK"][:, pb * TQ + ft * FT:
                                               pb * TQ + (ft + 1) * FT]
                                ).then_inc(sq.h, 16)
                Y(fld)
                n_l = sq.inc(16)

                def fmm(t, ft=ft, pb=pb, nb=nb, n_w=n_w,
                        pscs=pjm_scs[pb % 4], pve=pjm_ve[pb % 4]):
                    t.wait_ge(ve.h, nb)
                    t.wait_ge(gq.h, n_w)
                    t.wait_ge(scs.h, pscs)
                    t.wait_ge(ve.h, pve)
                    for cb in range(8):
                        mi = t.matmul(pj[pb % 4][:, :],
                                      wbuf[1][:, cb * D + pb * 128:
                                              cb * D + pb * 128 + 128],
                                      bl[ft % 2][:, cb * FT:(cb + 1) * FT],
                                      start=(cb == 0), stop=(cb == 7))
                    mi.then_inc(mm.h, 1)
                T(fmm)
                n_m = mm.inc(1)

                def fth(s, ft=ft, pb=pb, n_m=n_m, pve=ve.n):
                    s.wait_ge(mm.h, n_m)
                    s.wait_ge(ve.h, pve)
                    s.activation(stg[2][:, :], pj[pb % 4][:, :],
                                 AF.Tanh).then_inc(scs.h, 1)
                S(fth)
                n_t = scs.inc(1)

                def fev(v, ft=ft, pb=pb, n_t=n_t, n_l=n_l, psq=sq.n):
                    v.wait_ge(scs.h, n_t)
                    v.wait_ge(sq.h, max(n_l, psq))
                    v.tensor_tensor(stg[pb % 2][:, :], stg[2][:, :],
                                    stg[3][:, :], ALU.mult).then_inc(ve.h, 1)
                V(fev)
                n_e = ve.inc(1)
                spill(Y, s_strip["kh"], pb * TQ + ft * FT, stg[pb % 2],
                      ve, n_e)
                pjm_scs[pb % 4] = n_t
                pjm_ve[pb % 4] = ve.n
            bl_last_mm[ft % 2] = mm.n
            bl_last_ve[ft % 2] = ve.n
            wbuf_last_mm[1] = mm.n
    emit_wk()

    def emit_wab(widx, mixi, rawname, rscol):
        name = WN[widx]
        n_w = wsel_cycle[name]
        if widx + 1 < len(WN):
            wsel_cycle[WN[widx + 1]] = load_weight(
                WN[widx + 1], (widx + 1) % 2, wbuf_last_mm[(widx + 1) % 2])
        for ft in range(NFT):
            nb = blend_tiles(widx, mixi, ft, ft % 2, n_uall,
                             bl_last_mm[ft % 2])
            sq_mms = []
            for pb in range(8):
                def fmm(t, ft=ft, pb=pb, nb=nb, n_w=n_w,
                        pve=pjm_ve[pb % 4], psc=pjm_scs[pb % 4]):
                    t.wait_ge(ve.h, nb)
                    t.wait_ge(gq.h, n_w)
                    t.wait_ge(ve.h, pve)
                    t.wait_ge(scs.h, psc)
                    for cb in range(8):
                        mi = t.matmul(pj[pb % 4][:, :],
                                      wbuf[widx % 2][:, cb * D + pb * 128:
                                                     cb * D + pb * 128 + 128],
                                      bl[ft % 2][:, cb * FT:(cb + 1) * FT],
                                      start=(cb == 0), stop=(cb == 7))
                    mi.then_inc(mm.h, 1)
                T(fmm)
                n_m = mm.inc(1)

                def fev(v, ft=ft, pb=pb, n_m=n_m, psq=sq.n):
                    v.wait_ge(mm.h, n_m)
                    v.wait_ge(sq.h, psq)
                    vi = v.tensor_copy(stg[pb % 2][:, :], pj[pb % 4][:, :])
                    vi.then_inc(ve.h, 1)
                V(fev)
                n_e = ve.inc(1)
                spill(Y, s_strip[rawname], pb * TQ + ft * FT, stg[pb % 2],
                      ve, n_e)
                pjm_ve[pb % 4] = n_e

                def fsq2(s, pb=pb, n_e=n_e, pmm=mm.n):
                    s.wait_ge(ve.h, n_e)
                    s.wait_ge(mm.h, pmm)
                    s.activation(stg[2][:, :], stg[pb % 2][:, :],
                                 AF.Square).then_inc(scs.h, 1)
                S(fsq2)
                n_s2 = scs.inc(1)

                def fss(t, ft=ft, pb=pb, n_s2=n_s2):
                    t.wait_ge(scs.h, n_s2)
                    mi = t.matmul(pst[0:16, :],
                                  cgno[:, pb * 16:(pb + 1) * 16],
                                  stg[2][:, :],
                                  start=(pb == 0), stop=(pb == 7))
                    mi.then_inc(mm.h, 1)
                T(fss)
                sq_mms.append(mm.inc(1))

            def fcp(v, ft=ft, n=sq_mms[-1]):
                v.wait_ge(mm.h, n)
                vi = v.tensor_copy(statsB[0:16, ft * FT:(ft + 1) * FT],
                                   pst[0:16, :])
                vi.then_inc(ve.h, 1)
            V(fcp)
            ve.inc(1)
            bl_last_mm[0] = mm.n
            wbuf_last_mm[widx % 2] = mm.n

        def ffin_s(s, pv=ve.n):
            s.wait_ge(ve.h, pv)
            s.activation(statsB[0:16, :], statsB[0:16, :], AF.Ln)
            s.activation(statsB[0:16, :], statsB[0:16, :], AF.Exp,
                         scale=-0.5).then_inc(scs.h, 1)
        S(ffin_s)
        n_f = scs.inc(1)

        def ffin_v(v, rscol=rscol, n_f=n_f):
            v.wait_ge(scs.h, n_f)
            with nc.allow_low_precision(reason="rstd as bf16 matmul operand"):
                vi = v.tensor_scalar_min(rstdab[:, rscol:rscol + TQ],
                                         statsB[0:16, :], 1.0 / NORM_EPS)
            vi.then_inc(ve.h, 1)
        V(ffin_v)
        ve.inc(1)

    emit_wab(4, "mx_a", "araw", 0)
    emit_wab(5, "mx_b", "braw", TQ)

    def emit_hat(rawname, outname, ename, rscol):
        for pb in range(8):
            for ft in range(NFT):
                def fld(y, pb=pb, ft=ft, pv=ve.n, pgq=gq.n):
                    y.wait_ge(ve.h, pv)
                    y.wait_ge(gq.h, pgq)
                    y.dma_start(stg[0][:, :],
                                s_strip[rawname][:, pb * TQ + ft * FT:
                                                 pb * TQ + (ft + 1) * FT]
                                ).then_inc(sq.h, 16)
                    y.dma_start(stg[1][:, :],
                                s_strip[ename][:, pb * TQ + ft * FT:
                                               pb * TQ + (ft + 1) * FT]
                                ).then_inc(sq.h, 16)
                Y(fld)
                n_l = sq.inc(32)

                def fbc(t, pb=pb, ft=ft, pve=ve.n, rscol=rscol):
                    t.wait_ge(ve.h, pve)
                    mi = t.matmul(pbc[0][:, :],
                                  cgnb[:, pb * 128:(pb + 1) * 128],
                                  rstdab[:, rscol + ft * FT:
                                         rscol + (ft + 1) * FT],
                                  start=True, stop=True)
                    mi.then_inc(mm.h, 1)
                T(fbc)
                n_b = mm.inc(1)

                def fap(v, pb=pb, ft=ft, n_b=n_b, n_l=n_l, psq=sq.n):
                    v.wait_ge(mm.h, n_b)
                    v.wait_ge(sq.h, max(n_l, psq))
                    v.tensor_tensor(stg32[0][:, :], stg[0][:, :],
                                    pbc[0][:, :], ALU.mult)
                    vi = v.tensor_tensor(stg[2][:, :], stg32[0][:, :],
                                         stg[1][:, :], ALU.mult)
                    vi.then_inc(ve.h, 1)
                V(fap)
                n_ap = ve.inc(1)
                spill(Y, s_strip[outname], pb * TQ + ft * FT, stg[2],
                      ve, n_ap)

    emit_hat("araw", "ah", "eA", 0)
    emit_hat("braw", "bh", "eBK", TQ)

    def emit_wg():
        name = "W_g"
        n_w = wsel_cycle[name]
        wsel_cycle[WN[7]] = load_weight(WN[7], 1, wbuf_last_mm[1])
        for ft in range(NFT):
            nb = blend_tiles(6, "mx_g", ft, ft % 2, n_uall,
                             bl_last_mm[ft % 2])
            for pb in range(8):
                def fmm(t, ft=ft, pb=pb, nb=nb, n_w=n_w,
                        pscs=pjm_scs[pb % 4], pve=pjm_ve[pb % 4]):
                    t.wait_ge(ve.h, nb)
                    t.wait_ge(gq.h, n_w)
                    t.wait_ge(scs.h, pscs)
                    t.wait_ge(ve.h, pve)
                    for cb in range(8):
                        mi = t.matmul(pj[pb % 4][:, :],
                                      wbuf[0][:, cb * D + pb * 128:
                                              cb * D + pb * 128 + 128],
                                      bl[ft % 2][:, cb * FT:(cb + 1) * FT],
                                      start=(cb == 0), stop=(cb == 7))
                    mi.then_inc(mm.h, 1)
                T(fmm)
                n_m = mm.inc(1)

                def fev(s, ft=ft, pb=pb, n_m=n_m, psq=sq.n):
                    s.wait_ge(mm.h, n_m)
                    s.wait_ge(sq.h, psq)
                    s.activation(stg[pb % 2][:, :], pj[pb % 4][:, :],
                                 AF.Sigmoid).then_inc(scs.h, 1)
                S(fev)
                n_e = scs.inc(1)

                def fsp(y, ft=ft, pb=pb, n_e=n_e):
                    y.wait_ge(scs.h, n_e)
                    y.dma_start(s_strip["g"][:, pb * TQ + ft * FT:
                                             pb * TQ + (ft + 1) * FT],
                                stg[pb % 2][:, :]).then_inc(sq.h, 16)
                Y(fsp)
                sq.inc(16)
                pjm_scs[pb % 4] = n_e
            bl_last_mm[ft % 2] = mm.n
            bl_last_ve[ft % 2] = ve.n
            wbuf_last_mm[0] = mm.n
    emit_wg()

    n_p2_done = dict(sq=sq.n, gq=gq.n, mm=mm.n, ve=ve.n, scs=scs.n)

    def hs(tens, h, w=64, itemw=64):
        r0 = (h % 2) * 64
        c0 = (h // 2) * itemw
        return tens[r0:r0 + 64, c0:c0 + w]

    def sw(name, bi, h):
        r0 = (h % 2) * 64
        c0 = (h // 2) * C
        return stw[name][bi][r0:r0 + 64, c0:c0 + C]

    def pslot(bank, h):
        return bank[(h % 2) * 64:(h % 2) * 64 + 64,
                    (h // 2) * 64:(h // 2) * 64 + 64]

    def mask_ap(blk):
        return ap3(cmask, blk * C, [[4 * C, 128], [0, 8], [1, C]])

    def eh_ap(j):
        return ap3(ehcol, j, [[NDB * (NCH + 1), 128], [NCH + 1, 8], [0, 64]])

    def ee_ap(j):
        return ap3(eecol, j, [[NDB * NCH, 128], [NCH, 8], [0, 64]])

    def irep_ap():
        return ap3(cirep, 0, [[C, 128], [0, 8], [1, C]])

    def fdbge(g, pve=ve.n):
        g.wait_ge(ve.h, n_gamma)
        g.wait_ge(ve.h, pve)
        g.dma_start(s_dbg[:, 4 * CW:4 * CW + NDB * (NCH + 1)],
                    ehcol[:, :]).then_inc(gq.h, 16)
        g.dma_start(s_dbg[:, 5 * CW:5 * CW + NDB * NCH],
                    eecol[:, :]).then_inc(gq.h, 16)
    G(fdbge)
    gq.inc(32)

    def f3init(v, pve=ve.n):
        v.wait_ge(ve.h, n_gamma)
        v.wait_ge(scs.h, n_gamma_sc)
        v.wait_ge(ve.h, pve)
        v.memset(tcf[:, :], 0.0)
        v.memset(tcb[:, :], 0.0)
        v.tensor_tensor(xqf[:, :], irep_ap(), eh_ap(0), ALU.mult)
        vi = v.tensor_copy(xqb[:, :], xqf[:, :])
        vi.then_inc(ve.h, 1)
    V(f3init)
    ve.inc(1)

    last_ev = {i: (ve.n, scs.n) for i in range(4)}
    pre_banks = [pst, pex, pbc[0], pbc[1]]

    strip_loads = []
    p3_spill_gq = []

    def emit_strip_loads(j, pmm_consumed):
        bi = j % 2

        def fl(y, j=j, bi=bi, pmm=pmm_consumed):
            y.wait_ge(mm.h, pmm)
            for nstrip, sname in (("a", "ah"), ("b", "bh"), ("k", "kh"),
                                  ("r", "rh")):
                y.dma_start(
                    ap3(stw[nstrip][bi], 0, [[NDB * C, 128], [C, NDB], [1, C]]),
                    ap3(s_strip[sname], j * C, [[NDB * TQ, 128], [TQ, NDB],
                                                [1, C]])).then_inc(sq.h, 16)
            r0 = (64 * j) % 128
            cb = (j // 2) * D
            y.dma_start(vw[bi][0:64, :],
                        s_v[r0:r0 + 64, cb:cb + D]).then_inc(sq.h, 16)
            y.dma_start(vw[bi][64:128, :],
                        s_v[r0:r0 + 64, cb:cb + D]).then_inc(sq.h, 16)
        Y(fl)
        return sq.inc(16 * 6)

    strip_loads.append(emit_strip_loads(0, 0))
    chunk_mm_done = [0, 0]

    def pre_step(bank_i, stat_f, mov_f, n_ld, extra_mm_wait=0, fj=64):
        wv_, ws_ = last_ev[bank_i]

        def fmm(t, stat_f=stat_f, mov_f=mov_f, n_ld=n_ld, wv_=wv_, ws_=ws_,
                bank_i=bank_i, emw=extra_mm_wait):
            t.wait_ge(sq.h, n_ld)
            t.wait_ge(ve.h, wv_)
            t.wait_ge(scs.h, ws_)
            t.wait_ge(mm.h, emw)
            for h in range(H):
                mi = t.matmul(pslot(pre_banks[bank_i], h), stat_f(h),
                              mov_f(h), start=True, stop=True)
            mi.then_inc(mm.h, 1)
        T(fmm)
        return mm.inc(1)

    def pre_evac_mask(bank_i, dst, blk, n_mm):
        def fev(v, dst=dst, blk=blk, n_mm=n_mm, bank_i=bank_i):
            v.wait_ge(mm.h, n_mm)
            d3 = ap3(dst, 0, [[CW, 128], [64, 8], [1, 64]])
            p3_ = ap3(pre_banks[bank_i], 0, [[FT, 128], [64, 8], [1, 64]])
            vi = v.tensor_tensor(d3, p3_, mask_ap(blk), ALU.mult)
            vi.then_inc(ve.h, 1)
        V(fev)
        n = ve.inc(1)
        last_ev[bank_i] = (n, last_ev[bank_i][1])
        return n

    def pre_evac_copy(bank_i, dst_ap_f, n_mm):
        def fev(s, dst_ap_f=dst_ap_f, n_mm=n_mm, bank_i=bank_i):
            s.wait_ge(mm.h, n_mm)
            s.copy(dst_ap_f(), pre_banks[bank_i][:, :]).then_inc(scs.h, 1)
        S(fev)
        n = scs.inc(1)
        last_ev[bank_i] = (last_ev[bank_i][0], n)
        return n

    for j in range(NCH):
        bi = j % 2
        n_ld = strip_loads[j]
        if j + 1 < NCH:
            strip_loads.append(emit_strip_loads(j + 1, chunk_mm_done[(j + 1) % 2]))

        A = lambda h, bi=bi: sw("a", bi, h)
        B = lambda h, bi=bi: sw("b", bi, h)
        Kst = lambda h, bi=bi: sw("k", bi, h)
        R = lambda h, bi=bi: sw("r", bi, h)
        VW = lambda h, bi=bi: vw[bi][(h % 2) * 64:(h % 2) * 64 + 64,
                                     h * 64:h * 64 + 64]
        IR = lambda h: cirep[(h % 2) * 64:(h % 2) * 64 + 64, 0:C]

        n1 = pre_step(0, B, A, n_ld)
        pre_evac_mask(0, S3["tj"], 0, n1)
        n2 = pre_step(1, A, B, n_ld)
        pre_evac_mask(1, S3["aj"], 2, n2)
        n3 = pre_step(2, Kst, A, n_ld)
        pre_evac_mask(2, S3["nt"], 0, n3)
        n4 = pre_step(3, B, R, n_ld)
        def fmbbr_ev(v, n4=n4, pgq=gq.n):
            v.wait_ge(mm.h, n4)
            v.wait_ge(gq.h, pgq)
            d3 = ap3(S3["mbbr"], 0, [[CW, 128], [64, 8], [1, 64]])
            p3_ = ap3(pre_banks[3], 0, [[FT, 128], [64, 8], [1, 64]])
            vi = v.tensor_tensor(d3, p3_, mask_ap(1), ALU.mult)
            vi.then_inc(ve.h, 1)
        V(fmbbr_ev)
        _n = ve.inc(1)
        last_ev[3] = (_n, last_ev[3][1])
        n5 = pre_step(0, Kst, R, n_ld)
        pre_evac_mask(0, S3["mbkr"], 1, n5)
        n6 = pre_step(1, A, IR, n_ld)
        pre_evac_copy(1, lambda: S3["atok"][:, :], n6)
        n7 = pre_step(2, B, IR, n_ld)
        pre_evac_copy(2, lambda: S3["btk"][:, :], n7)
        n8 = pre_step(3, Kst, IR, n_ld)
        pre_evac_copy(3, lambda: S3["ktk"][:, :], n8)
        NT = lambda h: hs(S3["nt"], h)
        n9 = pre_step(0, NT, VW, n_ld)
        def fnv(s, n9=n9, j=j, pve=ve.n):
            s.wait_ge(mm.h, n9)
            s.wait_ge(ve.h, pve)
            s.copy(ap3(x0b, 0, [[2 * CW, 128], [128, 8], [1, 64]]),
                   ap3(pst, 0, [[FT, 128], [64, 8], [1, 64]]))
            s.copy(ap3(x0b, 64, [[2 * CW, 128], [128, 8], [1, 64]]),
                   ap3(S3["atok"], 0, [[CW, 128], [64, 8], [1, 64]])
                   ).then_inc(scs.h, 1)
        S(fnv)
        nnv = scs.inc(1)
        last_ev[0] = (last_ev[0][0], nnv)

        xcur, xnxt = x0b, x1b
        aj_t, tj_t, aj2_t, tj2_t = S3["aj"], S3["tj"], S3["aj2"], S3["tj2"]
        last_tj_t = tj_t
        for lev in range(6):
            TJ = lambda h, t_=tj_t: hs(t_, h)
            XC = lambda h, xcur=xcur: xcur[(h % 2) * 64:(h % 2) * 64 + 64,
                                           (h // 2) * 128:(h // 2) * 128 + 128]

            def fap(t, TJ=TJ, XC=XC,
                    pve=max(last_ev[0][0], last_ev[1][0]),
                    psc=max(last_ev[0][1], last_ev[1][1]),
                    ptj=scs.n):
                t.wait_ge(ve.h, pve)
                t.wait_ge(scs.h, max(psc, ptj))
                for h in range(H):
                    bank = pst if h < 8 else pex
                    hh = h if h < 8 else h - 8
                    mi = t.matmul(
                        bank[(h % 2) * 64:(h % 2) * 64 + 64,
                             (hh // 2) * 128:(hh // 2) * 128 + 128],
                        TJ(h), XC(h), start=True, stop=True)
                mi.then_inc(mm.h, 1)
            T(fap)
            nap = mm.inc(1)

            def fapev(v, xcur=xcur, xnxt=xnxt, nap=nap):
                v.wait_ge(mm.h, nap)
                v.tensor_tensor(xnxt[:, 0:CW], pst[:, :], xcur[:, 0:CW],
                                ALU.add)
                vi = v.tensor_tensor(xnxt[:, CW:2 * CW], pex[:, :],
                                     xcur[:, CW:2 * CW], ALU.add)
                vi.then_inc(ve.h, 1)
            V(fapev)
            nev = ve.inc(1)
            last_ev[0] = (nev, last_ev[0][1])
            last_ev[1] = (nev, last_ev[1][1])
            xcur, xnxt = xnxt, xcur

            if lev < 5:
                AJ = lambda h, t_=aj_t: hs(t_, h)
                TJc = lambda h, t_=tj_t: hs(t_, h)
                na = pre_step(2, TJc, AJ, n_ld)
                pre_evac_copy(2, lambda t_=aj2_t: t_[:, :], na)
                nt_ = pre_step(3, AJ, TJc, n_ld)
                pre_evac_copy(3, lambda t_=tj2_t: t_[:, :], nt_)
                aj_t, aj2_t = aj2_t, aj_t
                tj_t, tj2_t = tj2_t, tj_t

        xfin = xcur

        ZSL = lambda h, xfin=xfin: xfin[(h % 2) * 64:(h % 2) * 64 + 64,
                                        (h // 2) * 128 + 64:
                                        (h // 2) * 128 + 128]
        n10 = pre_step(2, ZSL, IR, n_ld)
        def fztt_ev(s, n10=n10, pgq=gq.n):
            s.wait_ge(mm.h, n10)
            s.wait_ge(gq.h, pgq)
            s.copy(S3["ztt"][:, :], pre_banks[2][:, :]).then_inc(scs.h, 1)
        S(fztt_ev)
        _n = scs.inc(1)
        last_ev[2] = (last_ev[2][0], _n)
        BTK = lambda h: hs(S3["btk"], h)
        n11 = pre_step(3, ZSL, BTK, n_ld)
        pre_evac_copy(3, lambda: S3["kt"][:, :], n11)

        wv0, ws0 = last_ev[0]
        wv1, ws1 = last_ev[1]

        def fsp3(g, j=j, ws0=ws0, wv1=wv1, pve=ve.n, psc=scs.n):
            g.wait_ge(scs.h, psc)
            g.wait_ge(ve.h, pve)
            g.dma_start(s_zt[:, j * CW:(j + 1) * CW],
                        S3["ztt"][:, :]).then_inc(gq.h, 16)
            g.dma_start(s_mbBR[:, j * CW:(j + 1) * CW],
                        S3["mbbr"][:, :]).then_inc(gq.h, 16)
            g.dma_start(s_btok[:, j * CW:(j + 1) * CW],
                        S3["btk"][:, :]).then_inc(gq.h, 16)
        G(fsp3)
        n_sp3 = gq.inc(48)

        ZTT = lambda h: hs(S3["ztt"], h)
        TCB = lambda h: hs(tcb, h)
        U0 = lambda h: hs(S3["u0"], h)
        MBBR = lambda h: hs(S3["mbbr"], h)
        MBKR = lambda h: hs(S3["mbkr"], h)
        KTK = lambda h: hs(S3["ktk"], h)
        KT = lambda h: hs(S3["kt"], h)
        XQ = lambda h: hs(xqb, h)

        wv0, ws0 = last_ev[0]

        def fu0(t, pve=ve.n, psc=scs.n, ZTT=ZTT, TCB=TCB):
            t.wait_ge(ve.h, pve)
            t.wait_ge(scs.h, psc)
            for h in range(H):
                mi = t.matmul(pslot(pj[0], h), ZTT(h), TCB(h),
                              start=True, stop=True)
            mi.then_inc(mm.h, 1)
        T(fu0)
        nu0 = mm.inc(1)

        def fu0ev(v, nu0=nu0, xfin=xfin):
            v.wait_ge(mm.h, nu0)
            vi = v.tensor_tensor(
                ap3(S3["u0"], 0, [[CW, 128], [64, 8], [1, 64]]),
                ap3(pj[0], 0, [[FT, 128], [64, 8], [1, 64]]),
                ap3(xfin, 0, [[2 * CW, 128], [128, 8], [1, 64]]), ALU.add)
            vi.then_inc(ve.h, 1)
        V(fu0ev)
        nu0e = ve.inc(1)

        def fo(t, nu0e=nu0e, TCB=TCB, R=R, U0=U0, MBBR=MBBR, VW=VW,
               MBKR=MBKR, pve=ve.n):
            t.wait_ge(ve.h, max(nu0e, pve))
            for h in range(H):
                t.matmul(pslot(pj[1], h), TCB(h), R(h), start=True, stop=False)
                t.matmul(pslot(pj[1], h), U0(h), MBBR(h), start=False,
                         stop=False)
                mi = t.matmul(pslot(pj[1], h), VW(h), MBKR(h), start=False,
                              stop=True)
            mi.then_inc(mm.h, 1)
        T(fo)
        no = mm.inc(1)

        def foev(s, no=no, j=j, pgq=gq.n):
            s.wait_ge(mm.h, no)
            s.wait_ge(gq.h, pgq)
            s.copy(oc[j % 2][:, :], pj[1][:, :]).then_inc(scs.h, 1)
        S(foev)
        n_oev = scs.inc(1)

        def fosp(g, j=j, n_oev=n_oev):
            g.wait_ge(scs.h, n_oev)
            g.dma_start(ap3(s_o, j * C, [[NDB * TQ, 128], [TQ, 8], [1, C]]),
                        oc[j % 2][:, :]).then_inc(gq.h, 16)
            g.dma_start(ap3(s_o0, j * C, [[NDB * TQ, 128], [TQ, 8], [1, C]]),
                        oc[j % 2][:, :]).then_inc(gq.h, 16)
        G(fosp)
        gq.inc(32)

        def fst3(t, nu0e=nu0e, BTK=BTK, U0=U0, KTK=KTK, VW=VW, pve=ve.n):
            t.wait_ge(ve.h, max(nu0e, pve))
            for h in range(H):
                t.matmul(pslot(pj[2], h), BTK(h), U0(h), start=True, stop=False)
                mi = t.matmul(pslot(pj[2], h), KTK(h), VW(h), start=False,
                              stop=True)
            mi.then_inc(mm.h, 1)
        T(fst3)
        nst = mm.inc(1)

        def fstev(v, nst=nst, j=j):
            v.wait_ge(mm.h, nst)
            v.tensor_tensor(tcf[:, :], pj[2][:, :], tcf[:, :], ALU.add)
            vi = v.tensor_tensor(tcf[:, :], tcf[:, :], ee_ap(j), ALU.mult)
            vi.then_inc(ve.h, 1)
        V(fstev)
        nstev = ve.inc(1)

        def fstc(s, nstev=nstev):
            s.wait_ge(ve.h, nstev)
            s.copy(tcb[:, :], tcf[:, :]).then_inc(scs.h, 1)
        S(fstc)
        scs.inc(1)

        def fkx(t, KT=KT, XQ=XQ, pve=ve.n, psc=scs.n):
            t.wait_ge(ve.h, pve)
            t.wait_ge(scs.h, psc)
            for h in range(H):
                mi = t.matmul(pslot(pj[3], h), KT(h), XQ(h), start=True,
                              stop=True)
            mi.then_inc(mm.h, 1)
        T(fkx)
        nkx = mm.inc(1)

        def fkxev(v, nkx=nkx, j=j):
            v.wait_ge(mm.h, nkx)
            v.tensor_tensor(xqf[:, :], pj[3][:, :], xqf[:, :], ALU.add)
            vi = v.tensor_tensor(xqf[:, :], xqf[:, :], ee_ap(j), ALU.mult)
            vi.then_inc(ve.h, 1)
        V(fkxev)
        nkxe = ve.inc(1)

        def fkxc(s, nkxe=nkxe):
            s.wait_ge(ve.h, nkxe)
            s.copy(xqb[:, :], xqf[:, :]).then_inc(scs.h, 1)
        S(fkxc)
        scs.inc(1)

        if j in (8, 10, 12, 14):
            slot_d = (j - 8) // 2
            n_fstc = scs.n

            def fdbgt(g, slot_d=slot_d, n_fstc=n_fstc):
                g.wait_ge(scs.h, n_fstc)
                g.dma_start(s_dbg[:, slot_d * CW:(slot_d + 1) * CW],
                            tcb[:, :]).then_inc(gq.h, 16)
            G(fdbgt)
            n_dump = gq.inc(16)

            def fbar(t, n_dump=n_dump):
                t.wait_ge(gq.h, n_dump)
            T(fbar)

        chunk_mm_done[bi] = mm.n

    def fsum(s, pve=ve.n):
        s.wait_ge(ve.h, pve)
        s.copy(cp_a[:, :], xqf[:, :])
        s.copy(cp_b[:, :], tcf[:, :]).then_inc(scs.h, 1)
    S(fsum)
    n_sum = scs.inc(1)

    def fdbg1(g, pve=ve.n, psc=scs.n):
        g.wait_ge(ve.h, pve)
        g.wait_ge(scs.h, psc)
        g.dma_start(s_dbg[:, 0:CW], tcf[:, :]).then_inc(gq.h, 16)
        g.dma_start(s_dbg[:, CW:2 * CW], xqf[:, :]).then_inc(gq.h, 16)
    G(fdbg1)
    gq.inc(32)

    n_gq_pre = gq.n

    def fsumd(g, n_sum=n_sum, n_gq_pre=n_gq_pre):
        g.wait_ge(scs.h, n_sum)
        g.dma_start(sum_loc[:, 0:CW], cp_a[:, :]).then_inc(gq.h, 16)
        g.dma_start(sum_loc[:, CW:2 * CW], cp_b[:, :]).then_inc(gq.h, 16)
        g.wait_ge(gq.h, n_gq_pre + 32)
        g.collective_compute(
            "AllGather", ALU.bypass, replica_groups=[list(range(8))],
            ins=[sum_loc.ap().opt()], outs=[sum_all.ap().opt()],
        ).then_inc(cc.h, 1)
    G(fsumd)
    gq.inc(32)
    cc.inc(1)

    cp_acc = cp_sin

    def fcpi(v, pcc=cc.n):
        v.wait_ge(cc.h, pcc)
        v.memset(tcf[:, :], 0.0)
        vi = v.memset(cp_sin[:, :], 0.0)
        vi.then_inc(ve.h, 1)
    V(fcpi)
    ve.inc(1)

    for chain_c in range(2):
        def fz(v, pve=ve.n):
            v.wait_ge(ve.h, pve)
            vi = v.memset(cp_sin[:, :], 0.0)
            vi.then_inc(ve.h, 1)
        V(fz)
        ve.inc(1)
        for p in range(4):
            cidx = chain_c * 4 + p
            def fsel(v, cidx=cidx, pve=ve.n):
                v.wait_ge(ve.h, pve)
                vi = v.scalar_tensor_tensor(tcf[:, :], cp_sin[:, :],
                                            ccsel[:, cidx:cidx + 1],
                                            tcf[:, :], ALU.mult, ALU.add)
                vi.then_inc(ve.h, 1)
            V(fsel)
            ve.inc(1)
            if p == 3:
                continue
            def fcl(y, cidx=cidx, pmm=mm.n, pve=ve.n):
                y.wait_ge(mm.h, pmm)
                y.wait_ge(ve.h, pve)
                y.dma_start(cp_a[:, :],
                            sum_all[cidx * 128:(cidx + 1) * 128, 0:CW]
                            ).then_inc(sq.h, 16)
                y.dma_start(cp_b[:, :],
                            sum_all[cidx * 128:(cidx + 1) * 128, CW:2 * CW]
                            ).then_inc(sq.h, 16)
            Y(fcl)
            n_cl = sq.inc(32)

            def fsb(s, pve=ve.n, n_cl=n_cl):
                s.wait_ge(ve.h, pve)
                s.copy(xqb[:, :], cp_sin[:, :]).then_inc(scs.h, 1)
            S(fsb)
            n_sb = scs.inc(1)

            def fq1(t, n_cl=n_cl, n_sb=n_sb, pve=ve.n, psc=scs.n):
                t.wait_ge(sq.h, n_cl)
                t.wait_ge(scs.h, max(n_sb, psc))
                t.wait_ge(ve.h, pve)
                for h in range(H):
                    mi = t.matmul(pslot(pst, h), hs(cp_a, h),
                                  cirep[(h % 2) * 64:(h % 2) * 64 + 64, 0:C],
                                  start=True, stop=True)
                mi.then_inc(mm.h, 1)
            T(fq1)
            n_q1 = mm.inc(1)

            def fq1e(s, n_q1=n_q1):
                s.wait_ge(mm.h, n_q1)
                s.copy(S3["kt"][:, :], pst[:, :]).then_inc(scs.h, 1)
            S(fq1e)
            n_q1e = scs.inc(1)

            def fq2(t, n_q1e=n_q1e):
                t.wait_ge(scs.h, n_q1e)
                for h in range(H):
                    mi = t.matmul(pslot(pex, h), hs(S3["kt"], h), hs(xqb, h),
                                  start=True, stop=True)
                mi.then_inc(mm.h, 1)
            T(fq2)
            n_q2 = mm.inc(1)

            def fq2e(v, n_q2=n_q2):
                v.wait_ge(mm.h, n_q2)
                vi = v.tensor_tensor(cp_sin[:, :], pex[:, :], cp_b[:, :],
                                     ALU.add)
                vi.then_inc(ve.h, 1)
            V(fq2e)
            ve.inc(1)

    def fdbg2(g, pve=ve.n):
        g.wait_ge(ve.h, pve)
        g.dma_start(s_dbg[:, 2 * CW:3 * CW], tcf[:, :]).then_inc(gq.h, 16)
    G(fdbg2)
    gq.inc(16)

    def f3ci(v, pve=ve.n, pgq=gq.n):
        v.wait_ge(ve.h, pve)
        v.wait_ge(gq.h, pgq)
        v.tensor_tensor(tcf[:, :], tcf[:, :], eh_ap(0), ALU.mult)
        vi = v.tensor_copy(tcb[:, :], tcf[:, :])
        vi.then_inc(ve.h, 1)
    V(f3ci)
    ve.inc(1)

    cw_loads = []

    def emit_cw_loads(j, pmm):
        bi = j % 2

        def fl(y, j=j, bi=bi, pmm=pmm, pgq=gq.n):
            y.wait_ge(mm.h, pmm)
            y.wait_ge(gq.h, pgq)
            y.dma_start(cwin["zt"][bi][:, :],
                        s_zt[:, j * CW:(j + 1) * CW]).then_inc(sq.h, 16)
            y.dma_start(cwin["br"][bi][:, :],
                        s_mbBR[:, j * CW:(j + 1) * CW]).then_inc(sq.h, 16)
            y.dma_start(cwin["bt"][bi][:, :],
                        s_btok[:, j * CW:(j + 1) * CW]).then_inc(sq.h, 16)
            y.dma_start(
                ap3(stw["r"][bi], 0, [[NDB * C, 128], [C, NDB], [1, C]]),
                ap3(s_strip["rh"], j * C, [[NDB * TQ, 128], [TQ, NDB],
                                           [1, C]])).then_inc(sq.h, 16)
            y.dma_start(oc[bi][:, :],
                        ap3(s_o, j * C, [[NDB * TQ, 128], [TQ, 8], [1, C]])
                        ).then_inc(sq.h, 16)
        Y(fl)
        return sq.inc(80)

    n_spill_all = gq.n
    cw_loads.append(emit_cw_loads(0, mm.n))
    cw_mm_done = [0, 0]
    for j in range(NCH):
        bi = j % 2
        n_ld = cw_loads[j]
        if j + 1 < NCH:
            cw_loads.append(emit_cw_loads(j + 1, cw_mm_done[(j + 1) % 2]))
        CZT = lambda h, bi=bi: hs(cwin["zt"][bi], h)
        CBR = lambda h, bi=bi: hs(cwin["br"][bi], h)
        CBT = lambda h, bi=bi: hs(cwin["bt"][bi], h)
        R3 = lambda h, bi=bi: sw("r", bi, h)
        TCB = lambda h: hs(tcb, h)
        U0 = lambda h: hs(S3["u0"], h)

        def fdu(t, n_ld=n_ld, CZT=CZT, TCB=TCB, pve=ve.n, psc=scs.n):
            t.wait_ge(sq.h, n_ld)
            t.wait_ge(ve.h, pve)
            t.wait_ge(scs.h, psc)
            for h in range(H):
                mi = t.matmul(pslot(pj[0], h), CZT(h), TCB(h), start=True,
                              stop=True)
            mi.then_inc(mm.h, 1)
        T(fdu)
        ndu = mm.inc(1)

        def fduev(s, ndu=ndu):
            s.wait_ge(mm.h, ndu)
            s.copy(S3["u0"][:, :], pj[0][:, :]).then_inc(scs.h, 1)
        S(fduev)
        ndue = scs.inc(1)

        def fo3(t, ndue=ndue, TCB=TCB, R3=R3, U0=U0, CBR=CBR, pve=ve.n):
            t.wait_ge(scs.h, ndue)
            t.wait_ge(ve.h, pve)
            for h in range(H):
                t.matmul(pslot(pj[1], h), TCB(h), R3(h), start=True,
                         stop=False)
                mi = t.matmul(pslot(pj[1], h), U0(h), CBR(h), start=False,
                              stop=True)
            mi.then_inc(mm.h, 1)
        T(fo3)
        no3 = mm.inc(1)

        def fo3ev(v, no3=no3, j=j, pgq=gq.n):
            v.wait_ge(mm.h, no3)
            v.wait_ge(gq.h, pgq)
            vi = v.tensor_tensor(oc[j % 2][:, :], pj[1][:, :],
                                 oc[j % 2][:, :], ALU.add)
            vi.then_inc(ve.h, 1)
        V(fo3ev)
        n_o3e = ve.inc(1)

        def fo3sp(g, j=j, n_o3e=n_o3e):
            g.wait_ge(ve.h, n_o3e)
            g.dma_start(ap3(s_o, j * C, [[NDB * TQ, 128], [TQ, 8], [1, C]]),
                        oc[j % 2][:, :]).then_inc(gq.h, 16)
        G(fo3sp)
        gq.inc(16)

        def fs3(t, ndue=ndue, CBT=CBT, U0=U0, pve=ve.n):
            t.wait_ge(scs.h, ndue)
            t.wait_ge(ve.h, pve)
            for h in range(H):
                mi = t.matmul(pslot(pj[2], h), CBT(h), U0(h), start=True,
                              stop=True)
            mi.then_inc(mm.h, 1)
        T(fs3)
        ns3 = mm.inc(1)

        def fs3ev(v, ns3=ns3, j=j):
            v.wait_ge(mm.h, ns3)
            v.tensor_tensor(tcf[:, :], pj[2][:, :], tcf[:, :], ALU.add)
            vi = v.tensor_tensor(tcf[:, :], tcf[:, :], ee_ap(j), ALU.mult)
            vi.then_inc(ve.h, 1)
        V(fs3ev)
        ns3e = ve.inc(1)

        def fs3c(s, ns3e=ns3e):
            s.wait_ge(ve.h, ns3e)
            s.copy(tcb[:, :], tcf[:, :]).then_inc(scs.h, 1)
        S(fs3c)
        scs.inc(1)
        cw_mm_done[bi] = mm.n

    wsel_cycle[WN[8]] = load_weight(WN[8], 0, wbuf_last_mm[0])
    GNS = 1.0 / 64.0
    for _b in range(4):
        pjm_ve[_b] = ve.n
        pjm_scs[_b] = scs.n

    for ft in range(NFT):
        def fol(y, ft=ft, pgq=gq.n, bmm=bl_last_mm[ft % 2],
                bve=bl_last_ve[ft % 2]):
            y.wait_ge(gq.h, pgq)
            y.wait_ge(mm.h, bmm)
            y.wait_ge(ve.h, bve)
            for pb in range(8):
                y.dma_start(bl[ft % 2][:, pb * FT:(pb + 1) * FT],
                            s_o[:, pb * TQ + ft * FT:pb * TQ + (ft + 1) * FT]
                            ).then_inc(sq.h, 16)
        Y(fol)
        n_ol = sq.inc(8 * 16)

        def fgs(t, ft=ft, n_ol=n_ol, pve=ve.n):
            t.wait_ge(sq.h, n_ol)
            t.wait_ge(ve.h, pve)
            for pb in range(8):
                mi = t.matmul(pst[0:16, :], cgno[:, pb * 16:(pb + 1) * 16],
                              bl[ft % 2][:, pb * FT:(pb + 1) * FT],
                              start=(pb == 0), stop=(pb == 7))
            mi.then_inc(mm.h, 1)
        T(fgs)
        mm.inc(1)

        n_qs = 0
        for pb in range(8):
            def fsqr(s, ft=ft, pb=pb, n_ol=n_ol, pmm=mm.n):
                s.wait_ge(sq.h, n_ol)
                s.wait_ge(mm.h, pmm)
                s.activation(stg[2 + (pb % 2)][:, :],
                             bl[ft % 2][:, pb * FT:(pb + 1) * FT],
                             AF.Square).then_inc(scs.h, 1)
            S(fsqr)
            n_sq2 = scs.inc(1)

            def fqs(t, pb=pb, n_sq2=n_sq2):
                t.wait_ge(scs.h, n_sq2)
                mi = t.matmul(pex[0:16, :], cgno[:, pb * 16:(pb + 1) * 16],
                              stg[2 + (pb % 2)][:, :],
                              start=(pb == 0), stop=(pb == 7))
                mi.then_inc(mm.h, 1)
            T(fqs)
            n_qs = mm.inc(1)

        def frows(v, n_qs=n_qs, ft=ft):
            v.wait_ge(mm.h, n_qs)
            sl = slice(ft * FT, (ft + 1) * FT)
            with nc.allow_low_precision(reason="gn mean bf16"):
                v.tensor_scalar_mul(rstdab[:, sl], pst[0:16, :], GNS)
            v.tensor_scalar_mul(spt[32:48, 0:FT], pex[0:16, :], GNS)
            v.tensor_tensor(spt[32:48, FT:2 * FT], rstdab[:, sl],
                            rstdab[:, sl], ALU.mult)
            v.tensor_scalar(spt[32:48, FT:2 * FT], spt[32:48, FT:2 * FT],
                            1.0, GN_EPS, ALU.mult, ALU.subtract)
            vi = v.tensor_tensor(spt[32:48, 0:FT], spt[32:48, 0:FT],
                                 spt[32:48, FT:2 * FT], ALU.subtract)
            vi.then_inc(ve.h, 1)
        V(frows)
        n_rows = ve.inc(1)

        def frstd(s, n_rows=n_rows, ft=ft):
            s.wait_ge(ve.h, n_rows)
            sl = slice(TQ + ft * FT, TQ + (ft + 1) * FT)
            s.activation(spt[32:48, 0:FT], spt[32:48, 0:FT], AF.Ln)
            with nc.allow_low_precision(reason="gn rstd bf16"):
                si = s.activation(rstdab[0:16, sl], spt[32:48, 0:FT], AF.Exp,
                                  scale=-0.5)
            si.then_inc(scs.h, 1)
        S(frstd)
        n_rstd = scs.inc(1)

        for pb in range(8):
            def fgl2(y, ft=ft, pb=pb, pve=ve.n):
                y.wait_ge(ve.h, pve)
                y.dma_start(stg[pb % 2][:, :],
                            s_strip["g"][:, pb * TQ + ft * FT:
                                         pb * TQ + (ft + 1) * FT]
                            ).then_inc(sq.h, 16)
            Y(fgl2)
            n_gl = sq.inc(16)

            def fbc2(t, ft=ft, pb=pb, n_rstd=n_rstd, pve=ve.n):
                t.wait_ge(scs.h, n_rstd)
                t.wait_ge(ve.h, pve)
                t.matmul(pbc[0][:, :], cgnb[:, pb * 128:(pb + 1) * 128],
                         rstdab[:, ft * FT:(ft + 1) * FT],
                         start=True, stop=True)
                mi = t.matmul(pbc[1][:, :], cgnb[:, pb * 128:(pb + 1) * 128],
                              rstdab[:, TQ + ft * FT:TQ + (ft + 1) * FT],
                              start=True, stop=True)
                mi.then_inc(mm.h, 1)
            T(fbc2)
            n_b2 = mm.inc(1)

            def fgate(v, ft=ft, pb=pb, n_b2=n_b2, n_gl=n_gl, psq=sq.n):
                v.wait_ge(mm.h, n_b2)
                v.wait_ge(sq.h, max(n_gl, psq))
                c0 = pb * FT
                v.tensor_tensor(stg32[pb % 2][:, :],
                                bl[ft % 2][:, c0:c0 + FT],
                                pbc[0][:, :], ALU.subtract)
                v.tensor_tensor(stg32[pb % 2][:, :], stg32[pb % 2][:, :],
                                pbc[1][:, :], ALU.mult)
                v.tensor_scalar(stg32[pb % 2][:, :], stg32[pb % 2][:, :],
                                cvec["gn_w"][:, pb:pb + 1],
                                cvec["gn_b"][:, pb:pb + 1],
                                ALU.mult, ALU.add)
                with nc.allow_low_precision(reason="gated bf16 operand"):
                    vi = v.tensor_tensor(bl[ft % 2][:, c0:c0 + FT],
                                         stg32[pb % 2][:, :],
                                         stg[pb % 2][:, :], ALU.mult)
                vi.then_inc(ve.h, 1)
            V(fgate)
            ve.inc(1)

        def fxl(y, ft=ft, pve=ve.n, psq=sq.n, pgq=gq.n):
            y.wait_ge(ve.h, pve)
            y.wait_ge(gq.h, pgq)
            y.dma_start(ap3(xt, 0, [[NDB * FT, 128], [FT, NDB], [1, FT]]),
                        xv[:, :, ft * FT:(ft + 1) * FT]).then_inc(sq.h, 16)
        Y(fxl)
        n_xl = sq.inc(16)

        gate_done = ve.n
        for pb in range(8):
            def fwo(t, ft=ft, pb=pb, gd=gate_done, pscs=scs.n,
                    pve=pjm_ve[pb % 4]):
                t.wait_ge(ve.h, max(gd, pve))
                t.wait_ge(scs.h, pscs)
                for cb in range(8):
                    mi = t.matmul(pj[pb % 4][:, :],
                                  wbuf[1][:, cb * D + pb * 128:
                                          cb * D + pb * 128 + 128],
                                  bl[ft % 2][:, cb * FT:(cb + 1) * FT],
                                  start=(cb == 0), stop=(cb == 7))
                mi.then_inc(mm.h, 1)
            T(fwo)
            n_wo = mm.inc(1)

            def fx1(v, ft=ft, pb=pb, n_wo=n_wo, n_xl=n_xl, pgq=gq.n):
                v.wait_ge(mm.h, n_wo)
                v.wait_ge(sq.h, n_xl)
                v.wait_ge(gq.h, pgq)
                c0 = pb * FT
                v.tensor_tensor(xt[:, c0:c0 + FT], pj[pb % 4][:, :],
                                xt[:, c0:c0 + FT], ALU.add)
                v.tensor_scalar(xt[:, c0:c0 + FT], xt[:, c0:c0 + FT],
                                cvec["b_o"][:, pb:pb + 1], 1.0,
                                ALU.add, ALU.mult)
                with nc.allow_low_precision(reason="x1 bf16 stats copy"):
                    vi = v.tensor_copy(xbf[:, c0:c0 + FT], xt[:, c0:c0 + FT])
                vi.then_inc(ve.h, 1)
            V(fx1)
            n_x1 = ve.inc(1)

            def fx1sp(g, ft=ft, pb=pb, n_x1=n_x1):
                g.wait_ge(ve.h, n_x1)
                g.dma_start(s_x1[pb * 128:(pb + 1) * 128,
                                 ft * FT:(ft + 1) * FT],
                            xt[:, pb * FT:(pb + 1) * FT]).then_inc(gq.h, 16)
            G(fx1sp)
            gq.inc(16)
            pjm_ve[pb % 4] = n_x1

        def fl2m(t, pve=ve.n, plnv=pjm_ve[2]):
            t.wait_ge(ve.h, pve)
            t.wait_ge(ve.h, plnv)
            for k in range(NDB):
                mi = t.matmul(pj[2][0:1, :], onescol[:, :],
                              xbf[:, k * FT:(k + 1) * FT],
                              start=(k == 0), stop=(k == NDB - 1))
            mi.then_inc(mm.h, 1)
        T(fl2m)
        n_l2m = mm.inc(1)

        def fl2sq(s, n_l2m=n_l2m):
            s.wait_ge(mm.h, n_l2m)
            s.activation(xbf[:, :], xbf[:, :], AF.Square).then_inc(scs.h, 1)
        S(fl2sq)
        n_l2sq = scs.inc(1)

        def fl2q(t, n_l2sq=n_l2sq, plnv=pjm_ve[3]):
            t.wait_ge(scs.h, n_l2sq)
            t.wait_ge(ve.h, plnv)
            for k in range(NDB):
                mi = t.matmul(pj[3][0:1, :], onescol[:, :],
                              xbf[:, k * FT:(k + 1) * FT],
                              start=(k == 0), stop=(k == NDB - 1))
            mi.then_inc(mm.h, 1)
        T(fl2q)
        n_l2q = mm.inc(1)

        def fl2st(v, ft=ft, n_l2q=n_l2q):
            v.wait_ge(mm.h, n_l2q)
            sl = slice(ft * FT, (ft + 1) * FT)
            v.tensor_scalar_mul(gct[0:1, sl], pj[2][0:1, :], 1.0 / D)
            v.tensor_scalar_mul(spt[0:1, 0:FT], pj[3][0:1, :], 1.0 / D)
            v.tensor_tensor(spt[0:1, FT:2 * FT], gct[0:1, sl],
                            gct[0:1, sl], ALU.mult)
            v.tensor_tensor(statsB[0:1, sl], spt[0:1, 0:FT],
                            spt[0:1, FT:2 * FT], ALU.subtract)
            vi = v.tensor_scalar_add(statsB[0:1, sl], statsB[0:1, sl],
                                     LN_EPS)
            vi.then_inc(ve.h, 1)
        V(fl2st)
        n_l2st = ve.inc(1)
        pjm_ve[2] = n_l2st
        pjm_ve[3] = n_l2st
        bl_last_mm[ft % 2] = mm.n
        bl_last_ve[ft % 2] = ve.n
        wbuf_last_mm[1] = mm.n

    wsel_cycle[WN[9]] = load_weight(WN[9], 1, wbuf_last_mm[1])

    def fl2f(s, pve=ve.n):
        s.wait_ge(ve.h, pve)
        s.activation(statsB[0:1, :], statsB[0:1, :], AF.Ln)
        s.activation(statsB[0:1, :], statsB[0:1, :], AF.Exp,
                     scale=-0.5).then_inc(scs.h, 1)
    S(fl2f)
    n_l2f = scs.inc(1)

    def fl2rc(v, n_l2f=n_l2f):
        v.wait_ge(scs.h, n_l2f)
        vi = v.tensor_tensor(gct[0:1, :], gct[0:1, :], statsB[0:1, :],
                             ALU.mult)
        vi.then_inc(ve.h, 1)
    V(fl2rc)
    n_l2rc = ve.inc(1)

    for ft in range(NFT):
        def fl2bc(t, ft=ft, n_l2rc=n_l2rc, pve=ve.n):
            t.wait_ge(ve.h, max(n_l2rc, pve))
            sl = slice(ft * FT, (ft + 1) * FT)
            t.matmul(pbc[0][:, :], onesrow[:, :], gct[0:1, sl],
                     start=True, stop=True)
            mi = t.matmul(pbc[1][:, :], onesrow[:, :], statsB[0:1, sl],
                          start=True, stop=True)
            mi.then_inc(mm.h, 1)
        T(fl2bc)
        n_lb = mm.inc(1)

        def fl2xl(y, ft=ft, pve=ve.n, pgq=gq.n):
            y.wait_ge(ve.h, pve)
            y.wait_ge(gq.h, pgq)
            y.dma_start(ap3(xt, 0, [[NDB * FT, 128], [FT, NDB], [1, FT]]),
                        x1v[:, :, ft * FT:(ft + 1) * FT]).then_inc(sq.h, 16)
        Y(fl2xl)
        n_lx = sq.inc(16)

        def fl2u(v, ft=ft, n_lb=n_lb, n_lx=n_lx):
            v.wait_ge(mm.h, n_lb)
            v.wait_ge(sq.h, n_lx)
            for k in range(NDB):
                v.tensor_tensor(stg32[0][:, :], xt[:, k * FT:(k + 1) * FT],
                                pbc[1][:, :], ALU.mult)
                with nc.allow_low_precision(reason="ln2 bf16"):
                    vi = v.tensor_tensor(u_ap(k, ft * FT, (ft + 1) * FT),
                                         stg32[0][:, :], pbc[0][:, :],
                                         ALU.subtract)
            vi.then_inc(ve.h, 1)
        V(fl2u)
        ve.inc(1)

    def fl2h(v, pve=ve.n):
        v.wait_ge(ve.h, pve)
        with nc.allow_low_precision(reason="ln2h bf16"):
            vi = v.tensor_copy(S3["aj"][:, 0:128],
                               ap3(ut, TQ, [[NDB * UW, 128], [0, 16],
                                            [UW, 8]]))
        vi.then_inc(ve.h, 1)
    V(fl2h)
    n_hx = ve.inc(1)

    n_gq_h = gq.n

    def fl2hd(g, n_hx=n_hx, n_gq_h=n_gq_h):
        g.wait_ge(ve.h, n_hx)
        g.dma_start(ln2h_loc[:, :], S3["aj"][:, 0:128]).then_inc(gq.h, 16)
        g.wait_ge(gq.h, n_gq_h + 16)
        g.collective_compute(
            "AllGather", ALU.bypass, replica_groups=[list(range(8))],
            ins=[ln2h_loc.ap().opt()], outs=[ln2h_all.ap().opt()],
        ).then_inc(cc.h, 1)
    G(fl2hd)
    gq.inc(16)
    n_cc2 = cc.inc(1)

    def fl2hl(y, n_cc2=n_cc2, pve=ve.n):
        y.wait_ge(cc.h, n_cc2)
        y.wait_ge(ve.h, pve)
        for c in range(8):
            dst = (cp_a if c < 4 else cp_b)
            y.dma_start(dst[:, (c % 4) * 128:(c % 4) * 128 + 128],
                        ln2h_all[c * 128:(c + 1) * 128, :]).then_inc(sq.h, 16)
    Y(fl2hl)
    n_hload = sq.inc(8 * 16)

    def fl2sel(v, n_hload=n_hload):
        v.wait_ge(sq.h, n_hload)
        v.memset(tcf[:, 0:128], 0.0)
        for c in range(8):
            src = (cp_a if c < 4 else cp_b)
            v.scalar_tensor_tensor(tcf[:, 0:128],
                                   src[:, (c % 4) * 128:(c % 4) * 128 + 128],
                                   chsel[:, c:c + 1], tcf[:, 0:128],
                                   ALU.mult, ALU.add)
        with nc.allow_low_precision(reason="halo bf16"):
            vi = v.tensor_copy(ap3(ut, 0, [[NDB * UW, 128], [UW, NDB]]),
                               tcf[:, 0:8])
        vi.then_inc(ve.h, 1)
    V(fl2sel)
    n_usel = ve.inc(1)

    rsig_marks = {}
    kact_marks = {}

    def ev_fr(ft, pb, n_m):
        def fe(s, ft=ft, pb=pb, n_m=n_m, psq=sq.n):
            s.wait_ge(mm.h, n_m)
            s.wait_ge(sq.h, psq)
            s.activation(stg[pb % 2][:, :], pj[pb % 4][:, :],
                         AF.Sigmoid).then_inc(scs.h, 1)
        S(fe)
        n_e = scs.inc(1)

        def fsp(y, ft=ft, pb=pb, n_e=n_e):
            y.wait_ge(scs.h, n_e)
            y.dma_start(s_rsig[:, pb * TQ + ft * FT:pb * TQ + (ft + 1) * FT],
                        stg[pb % 2][:, :]).then_inc(sq.h, 16)
        Y(fsp)
        sq.inc(16)
        rsig_marks[ft] = sq.n
    std_proj_mms(8, "omu_r", ev_fr, uw=n_usel)
    wsel_cycle[WN[10]] = load_weight(WN[10], 0, wbuf_last_mm[0])
    n_rsig_done = sq.n

    def ev_fk(ft, pb, n_m):
        def fe(s, ft=ft, pb=pb, n_m=n_m, psq=sq.n):
            s.wait_ge(mm.h, n_m)
            s.wait_ge(sq.h, psq)
            s.activation(stg32[0][:, :], pj[pb % 4][:, :], AF.Relu)
            s.activation(stg[pb % 2][:, :], stg32[0][:, :],
                         AF.Square).then_inc(scs.h, 1)
        S(fe)
        n_e = scs.inc(1)

        def fsp(y, ft=ft, pb=pb, n_e=n_e):
            y.wait_ge(scs.h, n_e)
            y.dma_start(s_kact[:, pb * TQ + ft * FT:pb * TQ + (ft + 1) * FT],
                        stg[pb % 2][:, :]).then_inc(sq.h, 16)
        Y(fsp)
        sq.inc(16)
        kact_marks[ft] = sq.n
    std_proj_mms(9, "omu_k", ev_fk, uw=n_usel)
    n_kact_done = sq.n

    n_w10 = wsel_cycle[WN[10]]
    for ft in range(NFT):
        def fkl(y, ft=ft, n_kd=kact_marks[ft], bmm=bl_last_mm[ft % 2],
                bve=bl_last_ve[ft % 2]):
            y.wait_ge(sq.h, n_kd)
            y.wait_ge(mm.h, bmm)
            y.wait_ge(ve.h, bve)
            for pb in range(8):
                y.dma_start(bl[ft % 2][:, pb * FT:(pb + 1) * FT],
                            s_kact[:, pb * TQ + ft * FT:
                                   pb * TQ + (ft + 1) * FT]).then_inc(sq.h, 16)
        Y(fkl)
        n_kl = sq.inc(8 * 16)

        def fxl2(y, ft=ft, pve=ve.n, poq=oq.n, n_rd=rsig_marks[ft]):
            y.wait_ge(ve.h, pve)
            y.wait_ge(oq.h, poq)
            y.wait_ge(sq.h, n_rd)
            y.dma_start(ap3(xt, 0, [[NDB * FT, 128], [FT, NDB], [1, FT]]),
                        x1v[:, :, ft * FT:(ft + 1) * FT]).then_inc(sq.h, 16)
        Y(fxl2)
        n_x2 = sq.inc(16)

        for pb in range(8):
            def frl(y, ft=ft, pb=pb, pve=ve.n):
                y.wait_ge(ve.h, pve)
                y.dma_start(stg[pb % 2][:, :],
                            s_rsig[:, pb * TQ + ft * FT:
                                   pb * TQ + (ft + 1) * FT]).then_inc(sq.h, 16)
            Y(frl)
            n_rl = sq.inc(16)

            def fvm(t, ft=ft, pb=pb, n_kl=n_kl, n_w10=n_w10,
                    pve=pjm_ve[pb % 4], pscs=scs.n):
                t.wait_ge(sq.h, n_kl)
                t.wait_ge(gq.h, n_w10)
                t.wait_ge(ve.h, pve)
                t.wait_ge(scs.h, pscs)
                for cb in range(8):
                    mi = t.matmul(pj[pb % 4][:, :],
                                  wbuf[0][:, cb * D + pb * 128:
                                          cb * D + pb * 128 + 128],
                                  bl[ft % 2][:, cb * FT:(cb + 1) * FT],
                                  start=(cb == 0), stop=(cb == 7))
                mi.then_inc(mm.h, 1)
            T(fvm)
            n_vm = mm.inc(1)

            def fout(v, ft=ft, pb=pb, n_vm=n_vm, n_rl=n_rl, n_x2=n_x2):
                v.wait_ge(mm.h, n_vm)
                v.wait_ge(sq.h, max(n_rl, n_x2))
                c0 = pb * FT
                v.tensor_tensor(stg32[pb % 2][:, :], pj[pb % 4][:, :],
                                stg[pb % 2][:, :], ALU.mult)
                vi = v.tensor_tensor(xt[:, c0:c0 + FT], stg32[pb % 2][:, :],
                                     xt[:, c0:c0 + FT], ALU.add)
                vi.then_inc(ve.h, 1)
            V(fout)
            n_ov = ve.inc(1)

            def fosp(y, ft=ft, pb=pb, n_ov=n_ov):
                y.wait_ge(ve.h, n_ov)
                y.dma_start(out_t[pb * 128:(pb + 1) * 128,
                                  ft * FT:(ft + 1) * FT],
                            xt[:, pb * FT:(pb + 1) * FT]).then_inc(oq.h, 16)
            Y(fosp)
            oq.inc(16)
            pjm_ve[pb % 4] = n_ov
        bl_last_mm[ft % 2] = mm.n
        bl_last_ve[ft % 2] = ve.n

    def ffinal(t, n_oq=oq.n):
        t.wait_ge(oq.h, n_oq)
    T(ffinal)

    def realize():
        blk = st.enter_context(nc.Block())

        @blk.sync
        def _(y):
            for f in P["sync"]:
                f(y)

        @blk.gpsimd
        def _(g):
            for f in P["gpsimd"]:
                f(g)

        @blk.tensor
        def _(t):
            for f in P["tensor"]:
                f(t)

        @blk.vector
        def _(v):
            for f in P["vector"]:
                f(v)

        @blk.scalar
        def _(s):
            for f in P["scalar"]:
                f(s)
        st.close()

    ctx = dict(
        nc=nc, P=P, st=st, realize=realize, sems=dict(sq=sq, gq=gq, mm=mm,
        ve=ve, scs=scs, cc=cc), n_p2_done=n_p2_done,
    )
    ctx.update({k: v for k, v in locals().items()
                if not k.startswith("_") and k not in ctx})
    return nc, ctx


def _host_inputs_for_core(core, xs, W):
    b, q = core // 4, core % 4
    x_fm = np.ascontiguousarray(xs[b, q * TQ:(q + 1) * TQ].T)
    if q == 0:
        halo = np.zeros((D,), np.float32)
    else:
        halo = np.ascontiguousarray(xs[b, q * TQ - 1])

    def v128(vec):
        return np.ascontiguousarray(vec.reshape(NDB, 128).T)

    m_strict = np.triu(np.ones((C, C), np.float32), 1)
    m_incl = np.triu(np.ones((C, C), np.float32), 0)
    i64 = np.eye(64, dtype=np.float32)
    irep = np.concatenate([i64, i64], axis=0)
    gnones = np.zeros((128, NDB * 16), np.float32)
    for j in range(NDB):
        gnones[0:64, j * 16 + 2 * j] = 1.0
        gnones[64:128, j * 16 + 2 * j + 1] = 1.0
    gnbc = np.zeros((16, NDB * 128), np.float32)
    for j in range(NDB):
        gnbc[2 * j, j * 128 + 0:j * 128 + 64] = 1.0
        gnbc[2 * j + 1, j * 128 + 64:j * 128 + 128] = 1.0
    csel = np.zeros((128, 8), np.float32)
    csel[:, core] = 1.0
    hsel = np.zeros((128, 8), np.float32)
    if q > 0:
        hsel[:, core - 1] = 1.0

    im = {
        "x_in": x_fm,
        "xhalo": v128(halo),
        "masks_in": np.tile(np.concatenate(
            [m_strict, m_incl, m_strict.T, m_incl.T], axis=1), (2, 1)),
        "irep_in": irep,
        "gnones_in": gnones,
        "gnbc_in": gnbc,
        "csel_in": csel,
        "hsel_in": hsel,
    }
    for n, key in [("mx_r", "x_r"), ("mx_w", "x_w"), ("mx_k", "x_k"),
                   ("mx_v", "x_v"), ("mx_a", "x_a"), ("mx_b", "x_b"),
                   ("mx_g", "x_g"), ("mu_r", "mu_r"), ("mu_k", "mu_k"),
                   ("gn_w", "gn_w"), ("gn_b", "gn_b"), ("b_o", "b_o")]:
        im["v_" + n] = v128(W[key])
    for wn in ["W_w", "W_v", "W_r", "W_k", "W_a", "W_b", "W_g", "W_o",
               "fW_r", "fW_k", "fW_v"]:
        im["w_" + wn] = np.ascontiguousarray(W[wn].T)
    return im


def kernel(**inputs):
    xs = np.asarray(inputs["L0_attn_x"], np.float32)
    W = {k.replace("L0_attn_", "").replace("L0_ffn_W", "fW").replace(
        "L0_ffn_", ""): np.asarray(v, np.float32)
        for k, v in inputs.items() if k != "L0_attn_x"}
    ims = [_host_inputs_for_core(c, xs, W) for c in range(8)]
    nc, ctx = build_program(dev_dump=False)
    ctx["realize"]()
    res = run_bass_kernel_spmd(nc, ims, core_ids=list(range(8)))
    out = np.zeros((2, 4 * TQ, D), np.float32)
    for c in range(8):
        b, q = c // 4, c % 4
        ot = np.asarray(res.results[c]["out_t"], np.float32)
        out[b, q * TQ:(q + 1) * TQ, :] = ot.T
    return out
